# version 07
# baseline (speedup 1.0000x reference)
import os
import numpy as np
import ml_dtypes
from contextlib import ExitStack

import concourse.bass as bass
import concourse.mybir as mybir
from concourse.bass_utils import run_bass_kernel_spmd

F32 = mybir.dt.float32
BF16 = mybir.dt.bfloat16
I32 = mybir.dt.int32
AF = mybir.ActivationFunctionType
ALU = mybir.AluOpType

NCORE = 8
D = 2048
KC = 16
T = 1024
TH = T + 3
THP = 1536
NH = 6
DH = 512
EPS_LN = 1e-5
EPS_RMS = 1e-5
TWO_PI = 6.283185307179586
MAGIC = 12582912.0
HALF_PI = 1.5707963267948966

C_EPSLN, C_NEGHALF, C_HALFPI, C_NEG1, C_INV512, C_EPSRMS = range(6)
NCONST = 6

SEMS = ["sy", "po", "so", "pe", "ve", "ac", "ez", "gs", "cv", "cc"]


class Prog:
    def __init__(self):
        detect = os.environ.get("K_RACE", "1") == "1"
        self.nc = bass.Bass("TRN2", target_bir_lowering=False,
                            detect_race_conditions=detect)
        self.st = ExitStack()
        self.cnt = {}
        self.sem = {}
        for s in SEMS:
            self.sem[s] = self.st.enter_context(self.nc.semaphore(s))
            self.cnt[s] = 0

    def inc(self, inst, name, n=1):
        inst.then_inc(self.sem[name], n)
        self.cnt[name] += n
        return self.cnt[name]

    def wait(self, eng, name, val=None):
        v = self.cnt[name] if val is None else val
        if v > 0:
            eng.wait_ge(self.sem[name], v)


def pieces(total, piece=512):
    out, o = [], 0
    while o < total:
        n = min(piece, total - o)
        out.append((o, n))
        o += n
    return out


def make_consts(nc):
    cvals = np.zeros((128, NCONST), np.float32)
    cvals[:, C_EPSLN] = EPS_LN
    cvals[:, C_NEGHALF] = -0.5
    cvals[:, C_HALFPI] = HALF_PI
    cvals[:, C_NEG1] = -1.0
    cvals[:, C_INV512] = 1.0 / 512.0
    cvals[:, C_EPSRMS] = EPS_RMS
    return nc.inline_tensor(cvals, "consts_h")


def emit_ln(p, x_h, consts, ones_f, ones_b, xn):
    nc = p.nc
    with (
        nc.sbuf_tensor("x_sb", [128, KC * TH], F32) as x_sb,
        nc.sbuf_tensor("xbs", [128, TH], BF16) as xbs,
        nc.sbuf_tensor("xb2s", [128, TH], BF16) as xb2s,
        nc.sbuf_tensor("rows", [1, 4 * TH], F32) as rows,
        nc.sbuf_tensor("tscr", [128, TH], F32) as tscr,
    ):
        def R(i):
            return rows[0:1, i * TH:(i + 1) * TH]

        with nc.Block() as blk:
            @blk.sync
            def _(sync):
                p.inc(sync.dma_start(out=x_sb[:].rearrange("p (kc t) -> p kc t", t=TH),
                                     in_=x_h.rearrange("(kc p) t -> kc p t", p=128).transpose([1, 0, 2])), "sy", 16)

        sy_x = p.cnt["sy"]

        with nc.psum_tensor("ps_stat", [1, 2 * THP], F32) as ps_stat:
            with nc.Block() as blk:
                pe_base = p.cnt["pe"]
                ve_marks = {}
                ac_marks = {}

                @blk.vector
                def _(ve):
                    p.wait(ve, "sy", sy_x)
                    for kc in range(KC):
                        if kc >= 1:
                            p.wait(ve, "pe", pe_base + kc)
                        sl = slice(kc * TH, (kc + 1) * TH)
                        ve_marks[kc] = p.inc(ve.tensor_copy(xbs[:, :], x_sb[:, sl]), "ve")

                @blk.scalar
                def _(ac):
                    p.wait(ac, "sy", sy_x)
                    for kc in range(KC):
                        if kc >= 1:
                            p.wait(ac, "pe", pe_base + kc)
                        sl = slice(kc * TH, (kc + 1) * TH)
                        ac_marks[kc] = p.inc(ac.activation(xb2s[:, :], x_sb[:, sl], AF.Square), "ac")

                @blk.tensor
                def _(te):
                    for kc in range(KC):
                        p.wait(te, "ve", ve_marks[kc])
                        p.wait(te, "ac", ac_marks[kc])
                        last = None
                        for which, src in ((0, xbs), (1, xb2s)):
                            for (o, n) in pieces(TH):
                                last = te.matmul(ps_stat[0:1, which * THP + o: which * THP + o + n],
                                                 ones_b[:, :], src[:, o:o + n],
                                                 start=(kc == 0), stop=(kc == KC - 1))
                        p.inc(last, "pe")

            pe_st = p.cnt["pe"]

            with nc.Block() as blk:
                @blk.vector
                def _(ve):
                    p.wait(ve, "pe", pe_st)
                    ve.tensor_scalar(R(0), ps_stat[0:1, 0:TH], 1.0 / D, None, ALU.mult)
                    ve.tensor_scalar(R(1), ps_stat[0:1, THP:THP + TH], 1.0 / D, None, ALU.mult)
                    ve.tensor_tensor(R(3), R(0), R(0), ALU.mult)
                    p.inc(ve.tensor_sub(R(1), R(1), R(3)), "ve")

        ve_var = p.cnt["ve"]

        with nc.Block() as blk:
            @blk.scalar
            def _(ac):
                p.wait(ac, "ve", ve_var)
                ac.activation(R(2), R(1), AF.Ln, bias=consts[0:1, C_EPSLN:C_EPSLN + 1])
                p.inc(ac.activation(R(2), R(2), AF.Exp,
                                    scale=consts[0:1, C_NEGHALF:C_NEGHALF + 1]), "ac")

        ac_rstd = p.cnt["ac"]

        with nc.Block() as blk:
            @blk.vector
            def _(ve):
                p.wait(ve, "ac", ac_rstd)
                p.inc(ve.tensor_tensor(R(3), R(0), R(2), ALU.mult), "ve")

        ve_mr = p.cnt["ve"]

        with nc.psum_tensor("ps_rep", [128, 2 * THP], F32) as ps_rep, nc.Block() as blk:
            @blk.tensor
            def _(te):
                p.wait(te, "ve", ve_mr)
                last = None
                for which, idx in ((0, 2), (1, 3)):
                    for (o, n) in pieces(TH):
                        last = te.matmul(ps_rep[:, which * THP + o: which * THP + o + n],
                                         ones_f[0:1, :],
                                         rows[0:1, idx * TH + o: idx * TH + o + n],
                                         start=True, stop=True)
                p.inc(last, "pe")

            pe_rep = p.cnt["pe"]

            @blk.vector
            def _(ve):
                p.wait(ve, "pe", pe_rep)
                last = None
                for kc in range(KC):
                    sl = slice(kc * TH, (kc + 1) * TH)
                    ve.tensor_tensor(tscr[:, :], x_sb[:, sl], ps_rep[:, 0:TH], ALU.mult)
                    last = ve.tensor_tensor(xn[:, sl], tscr[:, :], ps_rep[:, THP:THP + TH], ALU.subtract)
                p.inc(last, "ve")



def build_fused():
    p = Prog()
    nc = p.nc

    x_h = nc.dram_tensor("x", [D, TH], F32, kind="ExternalInput").ap()
    wqkv_h = nc.dram_tensor("wqkv", [NH, D, 1024], F32, kind="ExternalInput").ap()
    wg_h = nc.dram_tensor("wg", [NH, D, DH], F32, kind="ExternalInput").ap()
    wab_h = nc.dram_tensor("wab", [D, 12], F32, kind="ExternalInput").ap()
    taps_h = nc.dram_tensor("taps", [48, 128, 4], F32, kind="ExternalInput").ap()
    rmask_h = nc.dram_tensor("rmask", [128, NCORE], F32, kind="ExternalInput").ap()
    rmsw_h = nc.dram_tensor("rmsw", [128, 4], F32, kind="ExternalInput").ap()
    gsc_h = nc.dram_tensor("gsc", [128, 64], F32, kind="ExternalInput").ap()
    usc_h = nc.dram_tensor("usc", [128, 64], F32, kind="ExternalInput").ap()
    dsc_h = nc.dram_tensor("dsc", [128, 16], F32, kind="ExternalInput").ap()
    gate_h = nc.dram_tensor("gate_t", [32, 16, 128, 256], I32, kind="ExternalInput").ap()
    up_h = nc.dram_tensor("up_t", [32, 16, 128, 256], I32, kind="ExternalInput").ap()
    down_h = nc.dram_tensor("down_t", [4, 64, 128, 512], I32, kind="ExternalInput").ap()
    wo_h = nc.dram_tensor("wo_t", [4, 24, 128, 512], F32, kind="ExternalInput").ap()
    out_h = nc.dram_tensor("out", [D, T], F32, kind="ExternalOutput").ap()

    b_scr_t = nc.dram_tensor("b_scr", [128, 24 * T], BF16)
    sg_scr_t = nc.dram_tensor("sg_scr", [128, 24 * T], BF16)
    cc_in_t = nc.dram_tensor("cc_in", [129, 24], F32)
    cc_out_t = nc.dram_tensor("cc_out", [NCORE * 129, 24], F32)
    b_scr = b_scr_t.ap()
    sg_scr = sg_scr_t.ap()
    cc_in = cc_in_t.ap()
    cc_out = cc_out_t.ap()

    inv_freq = (1.0 / (10000.0 ** (np.arange(0, DH, 2, dtype=np.float64) / DH))).astype(np.float32)
    invf_c = nc.inline_tensor(np.ascontiguousarray(inv_freq.reshape(2, 128).T), "invf_h")
    ones_row_f = nc.inline_tensor(np.ones((1, 128), np.float32), "onesrow_h")
    ones_col_b = nc.inline_tensor(np.ones((128, 1), np.float32).astype(ml_dtypes.bfloat16), "onescol_h")
    const_h = make_consts(nc)

    st = p.st
    consts = st.enter_context(nc.sbuf_tensor("consts", [128, NCONST], F32))
    ones_f = st.enter_context(nc.sbuf_tensor("ones_f", [1, 128], F32))
    ones_b = st.enter_context(nc.sbuf_tensor("ones_b", [128, 1], BF16))
    xn = st.enter_context(nc.sbuf_tensor("xn", [128, KC * TH], BF16))
    alpha = st.enter_context(nc.sbuf_tensor("alpha_s", [NH, T], F32))
    a_row = st.enter_context(nc.sbuf_tensor("a_row", [1, T], F32))
    rmsw = st.enter_context(nc.sbuf_tensor("rmsw_s", [128, 4], F32))
    gsc = st.enter_context(nc.sbuf_tensor("gsc_s", [128, 64], F32))
    usc = st.enter_context(nc.sbuf_tensor("usc_s", [128, 64], F32))
    dscn = st.enter_context(nc.sbuf_tensor("dscn", [128, 16], F32))
    dsc = st.enter_context(nc.sbuf_tensor("dsc_s", [128, 16], F32))
    rmask = st.enter_context(nc.sbuf_tensor("rmask_s", [128, NCORE], F32))
    ysb = st.enter_context(nc.sbuf_tensor("ysb", [128, NCORE * 24], F32))
    qsb = st.enter_context(nc.sbuf_tensor("qsb", [1, NCORE * 24], F32))
    cmine = st.enter_context(nc.sbuf_tensor("cmine", [128, 24], F32))

    stA = ExitStack()
    invf = stA.enter_context(nc.sbuf_tensor("invf_s", [128, 2], F32))
    invf2p = stA.enter_context(nc.sbuf_tensor("invf2p", [128, 2], F32))
    taps = stA.enter_context(nc.sbuf_tensor("taps_s", [128, 48 * 4], F32))
    tapsn = stA.enter_context(nc.sbuf_tensor("tapsn", [128, 48 * 4], F32))
    wab = stA.enter_context(nc.sbuf_tensor("wab_s", [128, KC * 12], BF16))
    beta = stA.enter_context(nc.sbuf_tensor("beta_s", [NH, T], F32))
    b_row = stA.enter_context(nc.sbuf_tensor("b_row", [1, T], F32))
    ca_row = stA.enter_context(nc.sbuf_tensor("ca_row", [1, T], F32))
    a_row1 = stA.enter_context(nc.sbuf_tensor("a_row1", [1, T], F32))
    b_row1 = stA.enter_context(nc.sbuf_tensor("b_row1", [1, T], F32))
    ca_row1 = stA.enter_context(nc.sbuf_tensor("ca_row1", [1, T], F32))
    a_rowp = [a_row, a_row1]
    b_rowp = [b_row, b_row1]
    ca_rowp = [ca_row, ca_row1]
    y_sb = stA.enter_context(nc.sbuf_tensor("y_sb", [128, 24], F32))
    q_sb = stA.enter_context(nc.sbuf_tensor("q_sb", [1, 24], F32))

    with nc.Block() as blk:
        @blk.sync
        def _(sync):
            p.inc(sync.dma_start(out=consts[:], in_=const_h.ap()), "sy", 16)
            p.inc(sync.dma_start(out=invf[:], in_=invf_c.ap()), "sy", 16)
            p.inc(sync.dma_start(out=ones_f[:], in_=ones_row_f.ap()), "sy", 16)
            p.inc(sync.dma_start(out=ones_b[:], in_=ones_col_b.ap()), "sy", 16)
            p.inc(sync.dma_start(out=taps[:].rearrange("p (c j) -> p c j", j=4),
                                 in_=taps_h.transpose([1, 0, 2])), "sy", 16)
            p.inc(sync.dma_start(out=rmsw[:], in_=rmsw_h), "sy", 16)
            p.inc(sync.dma_start(out=gsc[:], in_=gsc_h), "sy", 16)
            p.inc(sync.dma_start(out=usc[:], in_=usc_h), "sy", 16)
            p.inc(sync.dma_start(out=dsc[:], in_=dsc_h), "sy", 16)
            p.inc(sync.dma_start(out=rmask[:], in_=rmask_h), "sy", 16)

        @blk.gpsimd
        def _(gp):
            p.inc(gp.dma_start(out=wab[:].rearrange("p (kc n) -> p kc n", n=12),
                               in_=wab_h.rearrange("(kc p) n -> kc p n", p=128).transpose([1, 0, 2])), "po", 16)

    sy_consts = p.cnt["sy"]

    with nc.Block() as blk:
        @blk.vector
        def _(ve):
            p.wait(ve, "sy", sy_consts)
            ve.tensor_scalar(tapsn[:], taps[:], -1.0, None, ALU.mult)
            ve.tensor_scalar(dscn[:], dsc[:], -128.0, None, ALU.mult)
            p.inc(ve.tensor_scalar(invf2p[:], invf[:], 1.0 / TWO_PI, None, ALU.mult), "ve")

    emit_ln(p, x_h, consts, ones_f, ones_b, xn)
    ve_xn = p.cnt["ve"]

    bst = {}
    sgst = {}

    wqkv = stA.enter_context(nc.sbuf_tensor("wqkv_s", [128, KC * 1024], BF16))
    wg = stA.enter_context(nc.sbuf_tensor("wg_s", [128, KC * DH], BF16))
    if True:
        wqkv_done = {}
        wg_done = {}

        with nc.Block() as blk:
            @blk.gpsimd
            def _(gp):
                wqkv_done[0] = p.inc(
                    gp.dma_start(out=wqkv[:].rearrange("p (kc n) -> p kc n", n=1024),
                                 in_=wqkv_h[0].rearrange("(kc p) n -> kc p n", p=128).transpose([1, 0, 2])), "po", 16)
                wg_done[0] = p.inc(
                    gp.dma_start(out=wg[:].rearrange("p (kc n) -> p kc n", n=DH),
                                 in_=wg_h[0].rearrange("(kc p) n -> kc p n", p=128).transpose([1, 0, 2])), "po", 16)

        with nc.psum_tensor("ps_ab", [NH, 2 * T], F32) as ps_ab, nc.Block() as blk:
            @blk.tensor
            def _(te):
                p.wait(te, "ve", ve_xn)
                p.wait(te, "po", 16)
                last = None
                for which in range(2):
                    for (o, n) in pieces(T):
                        for kc in range(KC):
                            last = te.matmul(ps_ab[0:NH, which * T + o: which * T + o + n],
                                             wab[:, kc * 12 + which * NH: kc * 12 + (which + 1) * NH],
                                             xn[:, kc * TH + 3 + o: kc * TH + 3 + o + n],
                                             start=(kc == 0), stop=(kc == KC - 1))
                p.inc(last, "pe")

            pe_ab = p.cnt["pe"]

            @blk.scalar
            def _(ac):
                p.wait(ac, "pe", pe_ab)
                ac.activation(alpha[:, :], ps_ab[0:NH, 0:T], AF.Sigmoid)
                p.inc(ac.activation(beta[:, :], ps_ab[0:NH, T:2 * T], AF.Sigmoid), "ac")

        ac_ab = p.cnt["ac"]

        with (
            nc.sbuf_tensor("z_sb", [128, 8 * TH], BF16) as z_sb,
            nc.sbuf_tensor("vc", [128, 4 * T], BF16) as vc,
            nc.sbuf_tensor("dpre", [128, 2 * T], BF16) as dpre,
            nc.sbuf_tensor("rrn", [128, T], F32) as rrn,
            nc.sbuf_tensor("scr2", [128, T], F32) as scr2,
            nc.sbuf_tensor("sinb", [128, 2 * T], F32) as sinb,
            nc.sbuf_tensor("cosb", [128, 2 * T], F32) as cosb,
            nc.sbuf_tensor("b_b", [128, 4 * T], BF16) as b_b,
            nc.sbuf_tensor("ab_sb", [128, T], F32) as ab_sb,
            nc.sbuf_tensor("u0", [128, T], F32) as u0,
            nc.sbuf_tensor("sg_sb", [128, 4 * T], BF16) as sg_sb,
        ):
            def tap_col(h, cc, j, neg=False):
                t = tapsn if neg else taps
                idx = (h * 8 + cc) * 4 + j
                return t[:, idx:idx + 1]

            h4p1 = {}
            pe_rep_h = {}
            ve_b_h = {}
            for h in range(NH):
                with nc.psum_tensor(f"ps_z{h}", [128, 2 * THP], F32) as ps_z, nc.Block() as blk:
                    pz_done = {}

                    @blk.tensor
                    def _(te):
                        p.wait(te, "po", wqkv_done[h])
                        if h == 0:
                            p.wait(te, "ve")
                        else:
                            p.wait(te, "cv", h4p1[h - 1])
                        p.wait(te, "gs")
                        for cc in range(8):
                            par = cc % 2
                            if cc >= 2:
                                p.wait(te, "ez", 8 * h + cc - 1)
                            last = None
                            for (o, n) in pieces(TH):
                                for kc in range(KC):
                                    last = te.matmul(ps_z[:, par * THP + o: par * THP + o + n],
                                                     wqkv[:, kc * 1024 + cc * 128: kc * 1024 + cc * 128 + 128],
                                                     xn[:, kc * TH + o: kc * TH + o + n],
                                                     start=(kc == 0), stop=(kc == KC - 1))
                            pz_done[cc] = p.inc(last, "pe")
                        p.wait(te, "ez", 8 * h + 8)

                    @blk.vector
                    def _(ve):
                        for cc in range(8):
                            par = cc % 2
                            p.wait(ve, "pe", pz_done[cc])
                            p.inc(ve.tensor_copy(z_sb[:, cc * TH:(cc + 1) * TH],
                                                 ps_z[:, par * THP: par * THP + TH]), "ez")

                    @blk.gpsimd
                    def _(gp):
                        if h + 1 < NH:
                            p.wait(gp, "pe", pz_done[7])
                            wqkv_done[h + 1] = p.inc(
                                gp.dma_start(out=wqkv[:].rearrange("p (kc n) -> p kc n", n=1024),
                                             in_=wqkv_h[h + 1].rearrange("(kc p) n -> kc p n", p=128).transpose([1, 0, 2])),
                                "po", 16)

                with nc.Block() as blk:
                    @blk.vector
                    def _(ve):
                        for jc in range(2):
                            dsl = slice(jc * T, (jc + 1) * T)

                            def zk(j, jc=jc):
                                return z_sb[:, (2 + jc) * TH + j: (2 + jc) * TH + j + T]

                            def zq(j, jc=jc):
                                return z_sb[:, (0 + jc) * TH + j: (0 + jc) * TH + j + T]

                            ve.tensor_scalar(dpre[:, dsl], zk(0), tap_col(h, 2 + jc, 0), None, ALU.mult)
                            for j in range(1, 4):
                                ve.scalar_tensor_tensor(dpre[:, dsl], zk(j), tap_col(h, 2 + jc, j),
                                                        dpre[:, dsl], ALU.mult, ALU.add)
                            for j in range(4):
                                ve.scalar_tensor_tensor(dpre[:, dsl], zq(j), tap_col(h, 0 + jc, j, neg=True),
                                                        dpre[:, dsl], ALU.mult, ALU.add)
                        for cc in range(4):
                            vsl = slice(cc * T, (cc + 1) * T)

                            def zv(j, cc=cc):
                                return z_sb[:, (4 + cc) * TH + j: (4 + cc) * TH + j + T]

                            ve.tensor_scalar(vc[:, vsl], zv(0), tap_col(h, 4 + cc, 0), None, ALU.mult)
                            for j in range(1, 4):
                                ve.scalar_tensor_tensor(vc[:, vsl], zv(j), tap_col(h, 4 + cc, j),
                                                        vc[:, vsl], ALU.mult, ALU.add)
                        last = None
                        for jc in range(2):
                            dsl = slice(jc * T, (jc + 1) * T)
                            ve.tensor_scalar(rrn[:, :], dpre[:, dsl], invf2p[:, jc:jc + 1], MAGIC,
                                             ALU.mult, ALU.add)
                            ve.tensor_scalar(rrn[:, :], rrn[:, :], MAGIC, None, ALU.subtract)
                            ve.tensor_scalar(rrn[:, :], rrn[:, :], TWO_PI, None, ALU.mult)
                            last = ve.scalar_tensor_tensor(sinb[:, dsl], dpre[:, dsl], invf[:, jc:jc + 1],
                                                           rrn[:, :], ALU.mult, ALU.subtract)
                        p.inc(last, "ve")

                ve_rr = p.cnt["ve"]

                with nc.Block() as blk:
                    @blk.scalar
                    def _(ac):
                        p.wait(ac, "ve", ve_rr)
                        last = None
                        for jc in range(2):
                            dsl = slice(jc * T, (jc + 1) * T)
                            ac.activation(cosb[:, dsl], sinb[:, dsl], AF.Abs)
                            ac.activation(cosb[:, dsl], cosb[:, dsl], AF.Sin,
                                          scale=consts[:, C_NEG1:C_NEG1 + 1],
                                          bias=consts[:, C_HALFPI:C_HALFPI + 1])
                            last = ac.activation(sinb[:, dsl], sinb[:, dsl], AF.Sin)
                        p.inc(last, "ac")

                ac_trig = p.cnt["ac"]

                a_r = a_rowp[h % 2]
                b_r = b_rowp[h % 2]
                ca_r = ca_rowp[h % 2]
                with nc.psum_tensor(f"ps_r{h}", [128, 2 * T], F32) as ps_r:
                    with nc.Block() as blk:
                        @blk.gpsimd
                        def _(gp):
                            p.wait(gp, "ac", ac_ab)
                            if h >= 2:
                                p.wait(gp, "ve", ve_b_h[h - 2])
                                p.wait(gp, "pe", pe_rep_h[h - 2])
                            p.inc(gp.dma_start(out=a_r[:], in_=alpha[h:h + 1, :]), "sy", 16)
                            p.inc(gp.dma_start(out=b_r[:], in_=beta[h:h + 1, :]), "sy", 16)

                    sy_rows = p.cnt["sy"]

                    with nc.Block() as blk:
                        @blk.tensor
                        def _(te):
                            p.wait(te, "sy", sy_rows)
                            p.wait(te, "gs")
                            last = None
                            for which, src in ((0, b_r), (1, a_r)):
                                for (o, n) in pieces(T):
                                    last = te.matmul(ps_r[:, which * T + o: which * T + o + n],
                                                     ones_f[0:1, :], src[0:1, o:o + n],
                                                     start=True, stop=True)
                            p.inc(last, "pe")

                        pe_rep = p.cnt["pe"]
                        pe_rep_h[h] = pe_rep

                        @blk.vector
                        def _(ve):
                            p.wait(ve, "ac", ac_trig)
                            p.wait(ve, "pe", pe_rep)
                            ve.tensor_tensor_scan(ca_r[0:1, :], a_r[0:1, :], a_r[0:1, :],
                                                  1.0, ALU.mult, ALU.bypass)
                            for cc in range(4):
                                ve.tensor_copy(q_sb[0:1, h * 4 + cc: h * 4 + cc + 1],
                                               ca_r[0:1, T - 1:T])
                            for jc in range(2):
                                dsl = slice(jc * T, (jc + 1) * T)
                                ve.tensor_tensor(cosb[:, dsl], cosb[:, dsl], ps_r[:, 0:T], ALU.mult)
                                ve.tensor_tensor(sinb[:, dsl], sinb[:, dsl], ps_r[:, 0:T], ALU.mult)
                            p.inc(ve.tensor_copy(ab_sb[:, :], ps_r[:, T:2 * T]), "cv")

                        h4p1[h] = p.cnt["cv"]

                        @blk.vector
                        def _(ve):
                            p.wait(ve, "so")
                            for jc in range(2):
                                dsl = slice(jc * T, (jc + 1) * T)
                                e = slice(jc * T, (jc + 1) * T)
                                o_ = slice((2 + jc) * T, (3 + jc) * T)
                                ve.tensor_tensor(rrn[:, :], vc[:, e], cosb[:, dsl], ALU.mult)
                                ve.tensor_tensor(scr2[:, :], vc[:, o_], sinb[:, dsl], ALU.mult)
                                ve.tensor_tensor(b_b[:, e], rrn[:, :], scr2[:, :], ALU.subtract)
                                ve.tensor_tensor(rrn[:, :], vc[:, o_], cosb[:, dsl], ALU.mult)
                                ve.tensor_tensor(scr2[:, :], vc[:, e], sinb[:, dsl], ALU.mult)
                                ve.tensor_tensor(b_b[:, o_], rrn[:, :], scr2[:, :], ALU.add)
                            last = None
                            for cc in range(4):
                                bsl = slice(cc * T, (cc + 1) * T)
                                ve.tensor_tensor_scan(u0[:, :], ab_sb[:, :], b_b[:, bsl],
                                                      0.0, ALU.mult, ALU.add)
                                last = ve.tensor_copy(y_sb[:, h * 4 + cc: h * 4 + cc + 1], u0[:, T - 1:T])
                            p.inc(last, "ve")

                        ve_b = p.cnt["ve"]
                        ve_b_h[h] = ve_b

                        @blk.sync
                        def _(sync):
                            p.wait(sync, "ve", ve_b)
                            bst[h] = p.inc(sync.dma_start(out=b_scr[:, h * 4 * T:(h * 4 + 4) * T], in_=b_b[:, :]), "so", 16)

                with nc.psum_tensor(f"ps_g{h}", [128, 2 * T], F32) as ps_g, nc.Block() as blk:
                    gc_done = {}

                    @blk.tensor
                    def _(te):
                        p.wait(te, "po", wg_done[h])
                        for cc in range(4):
                            par = cc % 2
                            if cc >= 2:
                                p.wait(te, "gs", 4 * h + cc - 1)
                            last = None
                            for (o, n) in pieces(T):
                                for kc in range(KC):
                                    last = te.matmul(ps_g[:, par * T + o: par * T + o + n],
                                                     wg[:, kc * DH + cc * 128: kc * DH + cc * 128 + 128],
                                                     xn[:, kc * TH + 3 + o: kc * TH + 3 + o + n],
                                                     start=(kc == 0), stop=(kc == KC - 1))
                            gc_done[cc] = p.inc(last, "pe")
                        p.wait(te, "gs", 4 * h + 4)

                    @blk.scalar
                    def _(ac):
                        p.wait(ac, "so")
                        for cc in range(4):
                            par = cc % 2
                            p.wait(ac, "pe", gc_done[cc])
                            p.inc(ac.activation(sg_sb[:, cc * T:(cc + 1) * T],
                                                ps_g[:, par * T: par * T + T], AF.Silu), "gs")

                    @blk.sync
                    def _(sync):
                        p.wait(sync, "gs", 4 * h + 4)
                        sgst[h] = p.inc(sync.dma_start(out=sg_scr[:, h * 4 * T:(h * 4 + 4) * T], in_=sg_sb[:, :]), "so", 16)

                    @blk.gpsimd
                    def _(gp):
                        if h + 1 < NH:
                            p.wait(gp, "pe", gc_done[3])
                            wg_done[h + 1] = p.inc(
                                gp.dma_start(out=wg[:].rearrange("p (kc n) -> p kc n", n=DH),
                                             in_=wg_h[h + 1].rearrange("(kc p) n -> kc p n", p=128).transpose([1, 0, 2])),
                                "po", 16)

        with nc.Block() as blk:
            @blk.sync
            def _(sync):
                p.wait(sync, "ve")
                p.inc(sync.dma_start(out=cc_in[0:128, :], in_=y_sb[:]), "so", 16)
                p.inc(sync.dma_start(out=cc_in[128:129, :], in_=q_sb[:]), "so", 16)

        so_ccin = p.cnt["so"]

        with nc.Block() as blk:
            @blk.gpsimd
            def _(gp):
                p.wait(gp, "so", so_ccin)
                gp.collective_compute(
                    "AllGather",
                    ALU.bypass,
                    replica_groups=[list(range(NCORE))],
                    ins=[cc_in[:]],
                    outs=[cc_out[:]],
                ).then_inc(p.sem["cc"], 1)

        with nc.Block() as blk:
            @blk.sync
            def _(sync):
                p.wait(sync, "cc", 1)
                for r in range(NCORE):
                    p.inc(sync.dma_start(out=ysb[:, r * 24:(r + 1) * 24],
                                         in_=cc_out[r * 129:r * 129 + 128, :]), "sy", 16)
                    p.inc(sync.dma_start(out=qsb[0:1, r * 24:(r + 1) * 24],
                                         in_=cc_out[r * 129 + 128:r * 129 + 129, :]), "sy", 16)

        sy_carry = p.cnt["sy"]

    stA.close()
    acc = st.enter_context(nc.sbuf_tensor("acc", [128, KC * T], F32))

    with (
        nc.sbuf_tensor("cbuf", [128, 2 * 24], F32) as cbuf,
        nc.psum_tensor("ps_q", [128, NCORE * 32], F32) as ps_q,
        nc.Block() as blk,
    ):
        @blk.tensor
        def _(te):
            p.wait(te, "sy", sy_carry)
            last = None
            for r in range(NCORE):
                last = te.matmul(ps_q[:, r * 32:r * 32 + 24],
                                 ones_f[0:1, :],
                                 qsb[0:1, r * 24:(r + 1) * 24],
                                 start=True, stop=True)
            p.inc(last, "pe")

        pe_q = p.cnt["pe"]

        @blk.vector
        def _(ve):
            p.wait(ve, "pe", pe_q)
            cprev, cnext = cbuf[:, 0:24], cbuf[:, 24:48]
            ve.memset(cprev, 0.0)
            ve.memset(cmine[:], 0.0)
            last = None
            for r in range(NCORE):
                last = ve.scalar_tensor_tensor(cmine[:], cprev, rmask[:, r:r + 1],
                                               cmine[:], ALU.mult, ALU.add)
                if r < NCORE - 1:
                    ve.tensor_tensor(cnext, cprev, ps_q[:, r * 32:r * 32 + 24], ALU.mult)
                    last = ve.tensor_add(cnext, cnext, ysb[:, r * 24:(r + 1) * 24])
                    cprev, cnext = cnext, cprev
            p.inc(last, "ve")

    bsg_done = {}
    sg_done = {}
    scan_done = {}
    o_done = {}
    with nc.sbuf_tensor("o_sb", [128, 24 * T], BF16) as o_sb:
        with (
            nc.sbuf_tensor("b_c0", [128, T], BF16) as b_c0,
            nc.sbuf_tensor("b_c1", [128, T], BF16) as b_c1,
            nc.sbuf_tensor("sg_f", [128, 4 * T], BF16) as sg_f,
            nc.sbuf_tensor("ab2_sb", [128, T], F32) as ab2_sb,
            nc.sbuf_tensor("u_sb", [128, 4 * T], F32) as u_sb,
            nc.sbuf_tensor("u2s", [128, T], BF16) as u2s,
            nc.sbuf_tensor("rrow", [1, T], F32) as rrow,
            nc.sbuf_tensor("oscr", [128, T], F32) as oscr,
        ):
            b_c = [b_c0, b_c1]

            with nc.Block() as blk:
                @blk.sync
                def _(sync):
                    p.wait(sync, "so", bst[0])
                    bsg_done[0] = p.inc(sync.dma_start(out=b_c[0][:], in_=b_scr[:, 0:T]), "sy", 16)
                    p.wait(sync, "so", sgst[0])
                    sg_done[0] = p.inc(sync.dma_start(out=sg_f[:], in_=sg_scr[:, 0:4 * T]), "sy", 16)

            for h in range(NH):
                with (
                    nc.psum_tensor(f"ps_r2{h}", [128, T], F32) as ps_r2,
                    nc.psum_tensor(f"ps_m{h}", [1, T], F32) as ps_m,
                ):
                    with nc.Block() as blk:
                        @blk.sync
                        def _(sync):
                            p.wait(sync, "ac", ac_ab)
                            p.wait(sync, "pe")
                            p.inc(sync.dma_start(out=a_row[:], in_=alpha[h:h + 1, :]), "sy", 16)

                    sy_ar = p.cnt["sy"]

                    with nc.Block() as blk:
                        @blk.tensor
                        def _(te):
                            p.wait(te, "sy", sy_ar)
                            p.wait(te, "ve")
                            last = None
                            for (o, n) in pieces(T):
                                last = te.matmul(ps_r2[:, o:o + n], ones_f[0:1, :],
                                                 a_row[0:1, o:o + n], start=True, stop=True)
                            p.inc(last, "pe")

                        pe_ar = p.cnt["pe"]

                        @blk.vector
                        def _(ve):
                            p.wait(ve, "pe", pe_ar)
                            p.inc(ve.tensor_copy(ab2_sb[:, :], ps_r2[:, :]), "ve")

                    for cc in range(4):
                        gi = h * 4 + cc
                        par = gi % 2
                        with nc.Block() as blk:
                            @blk.sync
                            def _(sync):
                                if gi + 1 < 24:
                                    if gi >= 1:
                                        p.wait(sync, "ve", scan_done[gi - 1])
                                    p.wait(sync, "so", bst[(gi + 1) // 4])
                                    bsg_done[gi + 1] = p.inc(
                                        sync.dma_start(out=b_c[(gi + 1) % 2][:],
                                                       in_=b_scr[:, (gi + 1) * T:(gi + 2) * T]), "sy", 16)

                            @blk.vector
                            def _(ve):
                                p.wait(ve, "sy", bsg_done[gi])
                                scan_done[gi] = p.inc(
                                    ve.tensor_tensor_scan(u_sb[:, cc * T:(cc + 1) * T],
                                                          ab2_sb[:, :], b_c[par][:, :],
                                                          cmine[:, gi:gi + 1],
                                                          ALU.mult, ALU.add), "ve")

                    ve_u = p.cnt["ve"]

                    with nc.Block() as blk:
                        pe_base = p.cnt["pe"]
                        gs_marks = {}

                        @blk.scalar
                        def _(ac):
                            p.wait(ac, "ve", ve_u)
                            for cc in range(4):
                                if cc >= 1:
                                    p.wait(ac, "pe", pe_base + cc)
                                gs_marks[cc] = p.inc(
                                    ac.activation(u2s[:, :], u_sb[:, cc * T:(cc + 1) * T], AF.Square), "gs")

                        @blk.tensor
                        def _(te):
                            for cc in range(4):
                                p.wait(te, "gs", gs_marks[cc])
                                last = None
                                for (o, n) in pieces(T):
                                    last = te.matmul(ps_m[0:1, o:o + n], ones_b[:, :],
                                                     u2s[:, o:o + n],
                                                     start=(cc == 0), stop=(cc == 3))
                                p.inc(last, "pe")

                    pe_m = p.cnt["pe"]

                    with nc.Block() as blk:
                        @blk.scalar
                        def _(ac):
                            p.wait(ac, "pe", pe_m)
                            ac.activation(rrow[0:1, :], ps_m[0:1, 0:T], AF.Ln,
                                          scale=consts[0:1, C_INV512:C_INV512 + 1],
                                          bias=consts[0:1, C_EPSRMS:C_EPSRMS + 1])
                            p.inc(ac.activation(rrow[0:1, :], rrow[0:1, :], AF.Exp,
                                                scale=consts[0:1, C_NEGHALF:C_NEGHALF + 1]), "ac")

                    ac_rs = p.cnt["ac"]

                    with nc.Block() as blk:
                        @blk.tensor
                        def _(te):
                            p.wait(te, "ac", ac_rs)
                            last = None
                            for (o, n) in pieces(T):
                                last = te.matmul(ps_r2[:, o:o + n], ones_f[0:1, :],
                                                 rrow[0:1, o:o + n], start=True, stop=True)
                            p.inc(last, "pe")

                        pe_rs = p.cnt["pe"]

                        @blk.vector
                        def _(ve):
                            p.wait(ve, "pe", pe_rs)
                            p.wait(ve, "sy", sg_done[h])
                            last = None
                            for cc in range(4):
                                gi = h * 4 + cc
                                ve.scalar_tensor_tensor(oscr[:, :], u_sb[:, cc * T:(cc + 1) * T],
                                                        rmsw[:, cc:cc + 1], ps_r2[:, :],
                                                        ALU.mult, ALU.mult)
                                last = ve.tensor_tensor(o_sb[:, gi * T:(gi + 1) * T],
                                                        oscr[:, :], sg_f[:, cc * T:(cc + 1) * T], ALU.mult)
                            o_done[h] = p.inc(last, "ve")

                        @blk.sync
                        def _(sync):
                            if h + 1 < NH:
                                p.wait(sync, "ve", o_done[h])
                                p.wait(sync, "so", sgst[h + 1])
                                sg_done[h + 1] = p.inc(
                                    sync.dma_start(out=sg_f[:],
                                                   in_=sg_scr[:, (h + 1) * 4 * T:(h + 2) * 4 * T]), "sy", 16)

        ve_o = p.cnt["ve"]

        wo_done = {}
        wo_pe_half = {}
        with (
            nc.sbuf_tensor("wo_sb0", [128, 12 * 512], BF16) as wo_sb0,
            nc.sbuf_tensor("wo_sb1", [128, 12 * 512], BF16) as wo_sb1,
            nc.sbuf_tensor("x2a", [128, T], F32) as x2a,
            nc.sbuf_tensor("x2b", [128, T], F32) as x2b,
        ):
            x2 = [x2a, x2b]
            wo_b = [wo_sb0, wo_sb1]

            with nc.Block() as blk:
                @blk.gpsimd
                def _(gp):
                    for ch in range(2):
                        wo_done[ch] = p.inc(
                            gp.dma_start(out=wo_b[ch][:].rearrange("p (kc n) -> p kc n", n=512),
                                         in_=wo_h[0, ch * 12:(ch + 1) * 12].transpose([1, 0, 2])),
                            "po", 16)

            cv_wo_base = p.cnt["cv"]
            for mcb in range(4):
                with nc.psum_tensor(f"ps_o{mcb}", [128, 4 * T], F32) as ps_o, nc.Block() as blk:
                    @blk.tensor
                    def _(te):
                        if mcb == 0:
                            p.wait(te, "ve", ve_o)
                        if mcb >= 1:
                            p.wait(te, "cv", cv_wo_base + 4 * mcb)
                        for half in range(2):
                            ch = 2 * mcb + half
                            p.wait(te, "po", wo_done[ch])
                            last = None
                            for mi in range(4):
                                for (o, n) in pieces(T):
                                    for k2 in range(12):
                                        kc = half * 12 + k2
                                        last = te.matmul(ps_o[:, mi * T + o: mi * T + o + n],
                                                         wo_b[ch % 2][:, k2 * 512 + mi * 128: k2 * 512 + mi * 128 + 128],
                                                         o_sb[:, kc * T + o: kc * T + o + n],
                                                         start=(kc == 0), stop=(kc == 23))
                            wo_pe_half[ch] = p.inc(last, "pe")

                    pe_o = p.cnt["pe"]

                    sy_x2m = {}

                    @blk.sync
                    def _(sync):
                        for mi in range(4):
                            mc = mcb * 4 + mi
                            if mc >= 2:
                                p.wait(sync, "cv", cv_wo_base + mc - 1)
                            sy_x2m[mi] = p.inc(sync.dma_start(out=x2[mi % 2][:],
                                                 in_=x_h[mc * 128:(mc + 1) * 128, 3:TH]), "sy", 16)

                    @blk.gpsimd
                    def _(gp):
                        for half in range(2):
                            ch = 2 * (mcb + 1) + half
                            if ch < 8:
                                p.wait(gp, "pe", wo_pe_half[ch - 2])
                                wo_done[ch] = p.inc(
                                    gp.dma_start(out=wo_b[ch % 2][:].rearrange("p (kc n) -> p kc n", n=512),
                                                 in_=wo_h[mcb + 1, half * 12:(half + 1) * 12].transpose([1, 0, 2])),
                                    "po", 16)

                    @blk.vector
                    def _(ve):
                        p.wait(ve, "pe", pe_o)
                        for mi in range(4):
                            mc = mcb * 4 + mi
                            p.wait(ve, "sy", sy_x2m[mi])
                            p.inc(ve.tensor_tensor(acc[:, mc * T:(mc + 1) * T],
                                                   x2[mi % 2][:, :], ps_o[:, mi * T: mi * T + T],
                                                   ALU.add), "cv")

    cv_wo_end = p.cnt["cv"]

    NIG = 32
    gu_done = {}
    d_done = {}
    pg_done_hist = {}
    ped_hist = {}
    so_out_base = p.cnt["so"]
    with (
        nc.sbuf_tensor("graw0", [128, 16 * 256], BF16) as graw0,
        nc.sbuf_tensor("graw1", [128, 16 * 256], BF16) as graw1,
        nc.sbuf_tensor("uraw0", [128, 16 * 256], BF16) as uraw0,
        nc.sbuf_tensor("uraw1", [128, 16 * 256], BF16) as uraw1,
        nc.sbuf_tensor("sgt", [128, 2 * T], BF16) as sgt,
        nc.sbuf_tensor("h_sb", [128, 16 * T], BF16) as h_sb,
        nc.sbuf_tensor("draw0", [128, 16 * 512], BF16) as draw0,
        nc.sbuf_tensor("draw1", [128, 16 * 512], BF16) as draw1,
    ):
        s_row = sgt[:].bitcast(F32)[0:1, 0:T]
        graw = [graw0, graw1]
        uraw = [uraw0, uraw1]
        draw = [draw0, draw1]
        outs = [graw0[:].bitcast(F32)[:, 0:T], uraw0[:].bitcast(F32)[:, 0:T]]

        with nc.Block() as blk:
            @blk.gpsimd
            def _(gp):
                p.inc(gp.dma_start(out=graw[0][:].rearrange("p (kc n) -> p kc n", n=256),
                                   in_=gate_h[0].transpose([1, 0, 2])), "po", 16)
                gu_done[0] = p.inc(gp.dma_start(out=uraw[0][:].rearrange("p (kc n) -> p kc n", n=256),
                                                in_=up_h[0].transpose([1, 0, 2])), "po", 16)

        ve_out_marks = {}

        def emit_down(s, mcb):
            dmai = s * 4 + mcb
            par = dmai % 2
            with nc.psum_tensor(f"ps_d{dmai}", [128, 2 * T], F32) as ps_d, nc.Block() as blk:
                pe_base = p.cnt["pe"]
                ve_prev = p.cnt["ve"]
                ve2_marks = {}

                @blk.gpsimd
                def _(gp):
                    if dmai >= 2:
                        p.wait(gp, "pe", ped_hist[dmai - 2])
                    d_done[dmai] = p.inc(
                        gp.dma_start(out=draw[par][:].rearrange("p (kc n) -> p kc n", n=512),
                                     in_=down_h[mcb, s * 16:(s + 1) * 16].transpose([1, 0, 2])),
                        "po", 16)

                @blk.vector
                def _(ve2):
                    for j in range(2):
                        p.wait(ve2, "pe", pe_base + j + 1)
                        last = None
                        for mi in (2 * j, 2 * j + 1):
                            mc = mcb * 4 + mi
                            asl = slice(mc * T, (mc + 1) * T)
                            if s < 3:
                                last = ve2.scalar_tensor_tensor(acc[:, asl],
                                                                ps_d[:, (mi % 2) * T:(mi % 2) * T + T],
                                                                dsc[:, mc:mc + 1], acc[:, asl],
                                                                ALU.mult, ALU.add)
                            else:
                                if mc >= 2:
                                    p.wait(ve2, "so", so_out_base + 16 * (mc - 1))
                                ve_out_marks[mc] = p.inc(
                                    ve2.scalar_tensor_tensor(outs[mi % 2],
                                                             ps_d[:, (mi % 2) * T:(mi % 2) * T + T],
                                                             dsc[:, mc:mc + 1], acc[:, asl],
                                                             ALU.mult, ALU.add), "ve")
                        if s < 3:
                            ve2_marks[j] = p.inc(last, "ve")
                        else:
                            ve2_marks[j] = ve_out_marks[mcb * 4 + 2 * j + 1]

                @blk.tensor
                def _(te):
                    p.wait(te, "po", d_done[dmai])
                    p.wait(te, "ve", ve_prev)
                    for j in range(2):
                        if j == 1:
                            p.wait(te, "ve", ve2_marks[0])
                        last = None
                        for mi in (2 * j, 2 * j + 1):
                            for (o, n) in pieces(T):
                                for kc2 in range(16):
                                    last = te.matmul(ps_d[:, (mi % 2) * T + o:(mi % 2) * T + o + n],
                                                     draw[par][:, kc2 * 512 + mi * 128: kc2 * 512 + mi * 128 + 128],
                                                     h_sb[:, kc2 * T + o: kc2 * T + o + n],
                                                     start=(kc2 == 0), stop=(kc2 == 15))
                        p.inc(last, "pe")

                if s == 3:
                    @blk.sync
                    def _(sync):
                        for mi in range(4):
                            mc = mcb * 4 + mi
                            p.wait(sync, "ve", ve_out_marks[mc])
                            p.inc(sync.dma_start(out=out_h[mc * 128:(mc + 1) * 128, :],
                                                 in_=outs[mi % 2]), "so", 16)

                ped_hist[dmai] = pe_base + 2

        hmark = {}
        ve_S = None
        with nc.psum_tensor("ps_S", [1, T], F32) as ps_S:
            for s in range(4):
                for i8 in range(8):
                    ig = s * 8 + i8
                    par = ig % 2
                    with nc.psum_tensor(f"ps_gu{ig}", [128, 2 * T], F32) as ps_gu, nc.Block() as blk:
                        pe_base = p.cnt["pe"]
                        pg = {(0, 0): pe_base + 1, (1, 0): pe_base + 2,
                              (0, 1): pe_base + 3, (1, 1): pe_base + 4}
                        gs_prev = p.cnt["gs"]
                        ve_prev = p.cnt["ve"]
                        gs_marks = {}
                        ve_marks = {}

                        @blk.scalar
                        def _(ac):
                            p.wait(ac, "ve", ve_prev)
                            for icp in range(2):
                                ic = 2 * ig + icp
                                p.wait(ac, "pe", pg[(0, icp)])
                                gs_marks[icp] = p.inc(
                                    ac.activation(sgt[:, icp * T:(icp + 1) * T],
                                                  ps_gu[:, 0:T], AF.Silu,
                                                  scale=gsc[:, ic:ic + 1]), "gs")

                        @blk.vector
                        def _(ve):
                            for icp in range(2):
                                ic = 2 * ig + icp
                                p.wait(ve, "gs", gs_marks[icp])
                                p.wait(ve, "pe", pg[(1, icp)])
                                ve_marks[icp] = p.inc(ve.scalar_tensor_tensor(
                                    h_sb[:, (i8 * 2 + icp) * T:(i8 * 2 + icp + 1) * T],
                                    sgt[:, icp * T:(icp + 1) * T], usc[:, ic:ic + 1],
                                    ps_gu[:, T:2 * T], ALU.mult, ALU.mult), "ve")

                        @blk.tensor
                        def _(te):
                            p.wait(te, "po", gu_done[ig])
                            if ig == 0:
                                p.wait(te, "cv", cv_wo_end)
                            for icp in range(2):
                                for which, w in ((0, graw[par]), (1, uraw[par])):
                                    if icp == 0:
                                        if which == 0:
                                            p.wait(te, "gs", gs_prev)
                                        else:
                                            p.wait(te, "ve", ve_prev)
                                    else:
                                        if which == 0:
                                            p.wait(te, "gs", gs_marks[0])
                                        else:
                                            p.wait(te, "ve", ve_marks[0])
                                    psl = which * T
                                    last = None
                                    for (o, n) in pieces(T):
                                        for kc in range(KC):
                                            last = te.matmul(ps_gu[:, psl + o: psl + o + n],
                                                             w[:, kc * 256 + icp * 128: kc * 256 + icp * 128 + 128],
                                                             xn[:, kc * TH + 3 + o: kc * TH + 3 + o + n],
                                                             start=(kc == 0), stop=(kc == KC - 1))
                                    p.inc(last, "pe")

                        @blk.gpsimd
                        def _(gp):
                            if ig + 1 < NIG:
                                if ig >= 1:
                                    p.wait(gp, "pe", pg_done_hist[ig - 1])
                                p.inc(gp.dma_start(out=graw[(ig + 1) % 2][:].rearrange("p (kc n) -> p kc n", n=256),
                                                   in_=gate_h[ig + 1].transpose([1, 0, 2])), "po", 16)
                                gu_done[ig + 1] = p.inc(
                                    gp.dma_start(out=uraw[(ig + 1) % 2][:].rearrange("p (kc n) -> p kc n", n=256),
                                                 in_=up_h[ig + 1].transpose([1, 0, 2])), "po", 16)

                    pg_done_hist[ig] = pe_base + 4

                hmark[s] = p.cnt["ve"]

                with nc.Block() as blk:
                    @blk.tensor
                    def _(te):
                        p.wait(te, "ve", hmark[s])
                        last = None
                        for kc2 in range(16):
                            for (o, n) in pieces(T):
                                last = te.matmul(ps_S[0:1, o:o + n], ones_b[:, :],
                                                 h_sb[:, kc2 * T + o: kc2 * T + o + n],
                                                 start=(s == 0 and kc2 == 0),
                                                 stop=(s == 3 and kc2 == 15))
                        if s == 3:
                            p.inc(last, "pe")

                if s < 3:
                    for mcb in range(4):
                        emit_down(s, mcb)
                else:
                    pe_S = p.cnt["pe"]
                    with nc.Block() as blk:
                        @blk.vector
                        def _(ve):
                            p.wait(ve, "pe", pe_S)
                            p.inc(ve.tensor_copy(s_row[0:1, :], ps_S[0:1, :]), "ve")
                    ve_S = p.cnt["ve"]

        with nc.psum_tensor("ps_srep", [128, T], F32) as ps_srep:
            with nc.Block() as blk:
                @blk.tensor
                def _(te):
                    p.wait(te, "ve", ve_S)
                    last = None
                    for (o, n) in pieces(T):
                        last = te.matmul(ps_srep[:, o:o + n], ones_f[0:1, :],
                                         s_row[0:1, o:o + n], start=True, stop=True)
                    p.inc(last, "pe")

                pe_srep = p.cnt["pe"]

                @blk.vector
                def _(ve):
                    p.wait(ve, "pe", pe_srep)
                    last = None
                    for mc in range(16):
                        last = ve.scalar_tensor_tensor(acc[:, mc * T:(mc + 1) * T],
                                                       ps_srep[:, :], dscn[:, mc:mc + 1],
                                                       acc[:, mc * T:(mc + 1) * T],
                                                       ALU.mult, ALU.add)
                    p.inc(last, "ve")

            for mcb in range(4):
                emit_down(3, mcb)

        with nc.Block() as blk:
            @blk.sync
            def _(sync):
                p.wait(sync, "so")

    p.st.close()
    return p.nc



def _prep(inputs):
    x = np.asarray(inputs["L0_attn_hidden_states"]).reshape(8192, D)
    xT = np.ascontiguousarray(x.T)
    x_cores = []
    for c in range(NCORE):
        lo = c * T - 3
        sl = xT[:, max(lo, 0): (c + 1) * T]
        if lo < 0:
            sl = np.concatenate([np.zeros((D, -lo), np.float32), sl], axis=1)
        x_cores.append(np.ascontiguousarray(sl))

    Wq = np.asarray(inputs["L0_attn_Wq"])
    Wk = np.asarray(inputs["L0_attn_Wk"])
    Wv = np.asarray(inputs["L0_attn_Wv"])
    Wg = np.asarray(inputs["L0_attn_Wg"])
    Wo = np.asarray(inputs["L0_attn_Wo"])
    cq = np.asarray(inputs["L0_attn_conv_q"])
    ck = np.asarray(inputs["L0_attn_conv_k"])
    cv = np.asarray(inputs["L0_attn_conv_v"])
    rms_w = np.asarray(inputs["L0_attn_rms_w"])

    even = np.arange(0, DH, 2)
    perm_head = np.concatenate([even, even + 1])
    wqkv = np.empty((NH, D, 1024), np.float32)
    wg = np.empty((NH, D, DH), np.float32)
    taps = np.empty((NH, 8, 128, 4), np.float32)
    for h in range(NH):
        base = h * DH
        wqkv[h] = np.concatenate([Wq[:, base + even], Wk[:, base + even],
                                  Wv[:, base + perm_head]], axis=1)
        wg[h] = Wg[:, base + perm_head]
        taps[h] = np.concatenate([cq[base + even], ck[base + even],
                                  cv[base + perm_head]], axis=0).reshape(8, 128, 4)
    taps = np.ascontiguousarray(taps.reshape(48, 128, 4))
    perm_full = np.concatenate([h * DH + perm_head for h in range(NH)])
    wo_p = Wo[perm_full, :]
    wo_t = np.ascontiguousarray(wo_p.reshape(24, 128, 4, 512).transpose(2, 0, 1, 3))
    wab = np.ascontiguousarray(np.concatenate([np.asarray(inputs["L0_attn_Wa"]),
                                               np.asarray(inputs["L0_attn_Wb"])], axis=1))
    rmsw_cols = np.ascontiguousarray(rms_w[perm_head].reshape(4, 128).T)

    gate_w = np.asarray(inputs["L0_ffn_gate_w"])
    up_w = np.asarray(inputs["L0_ffn_up_w"])
    down_w = np.asarray(inputs["L0_ffn_down_w"])
    gate_t = np.ascontiguousarray(gate_w.T.reshape(KC, 128, 32, 256).transpose(2, 0, 1, 3))
    up_t = np.ascontiguousarray(up_w.T.reshape(KC, 128, 32, 256).transpose(2, 0, 1, 3))
    down_t = np.ascontiguousarray(down_w.T.reshape(64, 128, 4, 512).transpose(2, 0, 1, 3))
    gsc = np.ascontiguousarray(np.asarray(inputs["L0_ffn_gate_scale"]).reshape(64, 128).T)
    usc = np.ascontiguousarray(np.asarray(inputs["L0_ffn_up_scale"]).reshape(64, 128).T)
    dsc = np.ascontiguousarray(np.asarray(inputs["L0_ffn_down_scale"]).reshape(16, 128).T)

    return dict(x_cores=x_cores, wqkv=wqkv, wg=wg, taps=taps, wab=wab,
                wo_t=wo_t, rmsw=rmsw_cols,
                gate_t=gate_t, up_t=up_t, down_t=down_t, gsc=gsc, usc=usc, dsc=dsc)


def kernel(**inputs):
    pr = _prep(inputs)

    nc = build_fused()
    eye = np.eye(NCORE, dtype=np.float32)
    ins = []
    for c in range(NCORE):
        rmask = np.ascontiguousarray(np.broadcast_to(eye[c], (128, NCORE)))
        ins.append({
            "x": pr["x_cores"][c], "wqkv": pr["wqkv"], "wg": pr["wg"],
            "wab": pr["wab"], "taps": pr["taps"],
            "rmask": rmask, "rmsw": pr["rmsw"],
            "gsc": pr["gsc"], "usc": pr["usc"], "dsc": pr["dsc"],
            "gate_t": pr["gate_t"], "up_t": pr["up_t"], "down_t": pr["down_t"],
            "wo_t": pr["wo_t"],
        })
    res = run_bass_kernel_spmd(nc, ins, core_ids=list(range(NCORE)))

    outs = [np.asarray(res.results[c]["out"]) for c in range(NCORE)]
    full = np.concatenate(outs, axis=1)
    return np.ascontiguousarray(full.T).reshape(1, 8192, D).astype(np.float32)


# version 09 (final)
# speedup vs baseline: 1.0635x; 1.0635x over previous
import os
import numpy as np
import ml_dtypes
from contextlib import ExitStack

import concourse.bass as bass
import concourse.mybir as mybir
from concourse.bass_utils import run_bass_kernel_spmd

F32 = mybir.dt.float32
BF16 = mybir.dt.bfloat16
I32 = mybir.dt.int32
AF = mybir.ActivationFunctionType
ALU = mybir.AluOpType

NCORE = 8
D = 2048
KC = 16
T = 1024
TH = T + 3
THP = 1536
NH = 6
DH = 512
EPS_LN = 1e-5
EPS_RMS = 1e-5
TWO_PI = 6.283185307179586
MAGIC = 12582912.0
HALF_PI = 1.5707963267948966

C_EPSLN, C_NEGHALF, C_HALFPI, C_NEG1, C_INV512, C_EPSRMS = range(6)
NCONST = 6

SEMS = ["sy", "po", "so", "pe", "ve", "ac", "ez", "gs", "cv", "cc"]


class Prog:
    def __init__(self):
        detect = os.environ.get("K_RACE", "1") == "1"
        self.nc = bass.Bass("TRN2", target_bir_lowering=False,
                            detect_race_conditions=detect)
        self.st = ExitStack()
        self.cnt = {}
        self.sem = {}
        for s in SEMS:
            self.sem[s] = self.st.enter_context(self.nc.semaphore(s))
            self.cnt[s] = 0

    def inc(self, inst, name, n=1):
        inst.then_inc(self.sem[name], n)
        self.cnt[name] += n
        return self.cnt[name]

    def wait(self, eng, name, val=None):
        v = self.cnt[name] if val is None else val
        if v > 0:
            eng.wait_ge(self.sem[name], v)


def pieces(total, piece=512):
    out, o = [], 0
    while o < total:
        n = min(piece, total - o)
        out.append((o, n))
        o += n
    return out


def make_consts(nc):
    cvals = np.zeros((128, NCONST), np.float32)
    cvals[:, C_EPSLN] = EPS_LN
    cvals[:, C_NEGHALF] = -0.5
    cvals[:, C_HALFPI] = HALF_PI
    cvals[:, C_NEG1] = -1.0
    cvals[:, C_INV512] = 1.0 / 512.0
    cvals[:, C_EPSRMS] = EPS_RMS
    return nc.inline_tensor(cvals, "consts_h")


def emit_ln(p, x_h, consts, ones_f, ones_b, xn):
    nc = p.nc
    with (
        nc.sbuf_tensor("x_sb", [128, KC * TH], F32) as x_sb,
        nc.sbuf_tensor("xbs", [128, TH], BF16) as xbs,
        nc.sbuf_tensor("xb2s", [128, TH], BF16) as xb2s,
        nc.sbuf_tensor("rows", [1, 4 * TH], F32) as rows,
        nc.sbuf_tensor("tscr", [128, TH], F32) as tscr,
    ):
        def R(i):
            return rows[0:1, i * TH:(i + 1) * TH]

        with nc.Block(no_gpsimd_drain=True) as blk:
            @blk.sync
            def _(sync):
                p.inc(sync.dma_start(out=x_sb[:].rearrange("p (kc t) -> p kc t", t=TH),
                                     in_=x_h.rearrange("(kc p) t -> kc p t", p=128).transpose([1, 0, 2])), "sy", 16)

        sy_x = p.cnt["sy"]

        with nc.psum_tensor("ps_stat", [1, 2 * THP], F32) as ps_stat:
            with nc.Block(no_gpsimd_drain=True) as blk:
                pe_base = p.cnt["pe"]
                ve_marks = {}
                ac_marks = {}

                @blk.vector
                def _(ve):
                    p.wait(ve, "sy", sy_x)
                    for kc in range(KC):
                        if kc >= 1:
                            p.wait(ve, "pe", pe_base + kc)
                        sl = slice(kc * TH, (kc + 1) * TH)
                        ve_marks[kc] = p.inc(ve.tensor_copy(xbs[:, :], x_sb[:, sl]), "ve")

                @blk.scalar
                def _(ac):
                    p.wait(ac, "sy", sy_x)
                    for kc in range(KC):
                        if kc >= 1:
                            p.wait(ac, "pe", pe_base + kc)
                        sl = slice(kc * TH, (kc + 1) * TH)
                        ac_marks[kc] = p.inc(ac.activation(xb2s[:, :], x_sb[:, sl], AF.Square), "ac")

                @blk.tensor
                def _(te):
                    for kc in range(KC):
                        p.wait(te, "ve", ve_marks[kc])
                        p.wait(te, "ac", ac_marks[kc])
                        last = None
                        for which, src in ((0, xbs), (1, xb2s)):
                            for (o, n) in pieces(TH):
                                last = te.matmul(ps_stat[0:1, which * THP + o: which * THP + o + n],
                                                 ones_b[:, :], src[:, o:o + n],
                                                 start=(kc == 0), stop=(kc == KC - 1))
                        p.inc(last, "pe")

            pe_st = p.cnt["pe"]

            with nc.Block(no_gpsimd_drain=True) as blk:
                @blk.vector
                def _(ve):
                    p.wait(ve, "pe", pe_st)
                    ve.tensor_scalar(R(0), ps_stat[0:1, 0:TH], 1.0 / D, None, ALU.mult)
                    ve.tensor_scalar(R(1), ps_stat[0:1, THP:THP + TH], 1.0 / D, None, ALU.mult)
                    ve.tensor_tensor(R(3), R(0), R(0), ALU.mult)
                    p.inc(ve.tensor_sub(R(1), R(1), R(3)), "ve")

        ve_var = p.cnt["ve"]

        with nc.Block(no_gpsimd_drain=True) as blk:
            @blk.scalar
            def _(ac):
                p.wait(ac, "ve", ve_var)
                ac.activation(R(2), R(1), AF.Ln, bias=consts[0:1, C_EPSLN:C_EPSLN + 1])
                p.inc(ac.activation(R(2), R(2), AF.Exp,
                                    scale=consts[0:1, C_NEGHALF:C_NEGHALF + 1]), "ac")

        ac_rstd = p.cnt["ac"]

        with nc.Block(no_gpsimd_drain=True) as blk:
            @blk.vector
            def _(ve):
                p.wait(ve, "ac", ac_rstd)
                p.inc(ve.tensor_tensor(R(3), R(0), R(2), ALU.mult), "ve")

        ve_mr = p.cnt["ve"]

        with nc.psum_tensor("ps_rep", [128, 2 * THP], F32) as ps_rep, nc.Block(no_gpsimd_drain=True) as blk:
            @blk.tensor
            def _(te):
                p.wait(te, "ve", ve_mr)
                last = None
                for which, idx in ((0, 2), (1, 3)):
                    for (o, n) in pieces(TH):
                        last = te.matmul(ps_rep[:, which * THP + o: which * THP + o + n],
                                         ones_f[0:1, :],
                                         rows[0:1, idx * TH + o: idx * TH + o + n],
                                         start=True, stop=True)
                p.inc(last, "pe")

            pe_rep = p.cnt["pe"]

            @blk.vector
            def _(ve):
                p.wait(ve, "pe", pe_rep)
                last = None
                for kc in range(KC):
                    sl = slice(kc * TH, (kc + 1) * TH)
                    ve.tensor_tensor(tscr[:, :], x_sb[:, sl], ps_rep[:, 0:TH], ALU.mult)
                    last = ve.tensor_tensor(xn[:, sl], tscr[:, :], ps_rep[:, THP:THP + TH], ALU.subtract)
                p.inc(last, "ve")



def build_fused():
    p = Prog()
    nc = p.nc

    x_h = nc.dram_tensor("x", [D, TH], F32, kind="ExternalInput").ap()
    wqkv_h = nc.dram_tensor("wqkv", [NH, D, 1024], F32, kind="ExternalInput").ap()
    wg_h = nc.dram_tensor("wg", [NH, D, DH], F32, kind="ExternalInput").ap()
    wab_h = nc.dram_tensor("wab", [D, 12], F32, kind="ExternalInput").ap()
    taps_h = nc.dram_tensor("taps", [48, 128, 4], F32, kind="ExternalInput").ap()
    rmask_h = nc.dram_tensor("rmask", [128, NCORE], F32, kind="ExternalInput").ap()
    rmsw_h = nc.dram_tensor("rmsw", [128, 4], F32, kind="ExternalInput").ap()
    gsc_h = nc.dram_tensor("gsc", [128, 64], F32, kind="ExternalInput").ap()
    usc_h = nc.dram_tensor("usc", [128, 64], F32, kind="ExternalInput").ap()
    dsc_h = nc.dram_tensor("dsc", [128, 16], F32, kind="ExternalInput").ap()
    gate_h = nc.dram_tensor("gate_t", [32, 16, 128, 256], I32, kind="ExternalInput").ap()
    up_h = nc.dram_tensor("up_t", [32, 16, 128, 256], I32, kind="ExternalInput").ap()
    down_h = nc.dram_tensor("down_t", [4, 64, 128, 512], I32, kind="ExternalInput").ap()
    wo_h = nc.dram_tensor("wo_t", [4, 24, 128, 512], F32, kind="ExternalInput").ap()
    out_h = nc.dram_tensor("out", [D, T], F32, kind="ExternalOutput").ap()

    b_scr_t = nc.dram_tensor("b_scr", [128, 24 * T], BF16)
    sg_scr_t = nc.dram_tensor("sg_scr", [128, 24 * T], BF16)
    cc_in_t = nc.dram_tensor("cc_in", [129, 24], F32)
    cc_out_t = nc.dram_tensor("cc_out", [NCORE * 129, 24], F32)
    b_scr = b_scr_t.ap()
    sg_scr = sg_scr_t.ap()
    cc_in = cc_in_t.ap()
    cc_out = cc_out_t.ap()

    inv_freq = (1.0 / (10000.0 ** (np.arange(0, DH, 2, dtype=np.float64) / DH))).astype(np.float32)
    invf_c = nc.inline_tensor(np.ascontiguousarray(inv_freq.reshape(2, 128).T), "invf_h")
    ones_row_f = nc.inline_tensor(np.ones((1, 128), np.float32), "onesrow_h")
    ones_col_b = nc.inline_tensor(np.ones((128, 1), np.float32).astype(ml_dtypes.bfloat16), "onescol_h")
    const_h = make_consts(nc)

    st = p.st
    consts = st.enter_context(nc.sbuf_tensor("consts", [128, NCONST], F32))
    ones_f = st.enter_context(nc.sbuf_tensor("ones_f", [1, 128], F32))
    ones_b = st.enter_context(nc.sbuf_tensor("ones_b", [128, 1], BF16))
    xn = st.enter_context(nc.sbuf_tensor("xn", [128, KC * TH], BF16))
    alpha = st.enter_context(nc.sbuf_tensor("alpha_s", [NH, T], F32))
    a_row = st.enter_context(nc.sbuf_tensor("a_row", [1, T], F32))
    rmsw = st.enter_context(nc.sbuf_tensor("rmsw_s", [128, 4], F32))
    gsc = st.enter_context(nc.sbuf_tensor("gsc_s", [128, 64], F32))
    usc = st.enter_context(nc.sbuf_tensor("usc_s", [128, 64], F32))
    dscn = st.enter_context(nc.sbuf_tensor("dscn", [128, 16], F32))
    dsc = st.enter_context(nc.sbuf_tensor("dsc_s", [128, 16], F32))
    rmask = st.enter_context(nc.sbuf_tensor("rmask_s", [128, NCORE], F32))
    ysb = st.enter_context(nc.sbuf_tensor("ysb", [128, NCORE * 24], F32))
    qsb = st.enter_context(nc.sbuf_tensor("qsb", [1, NCORE * 24], F32))
    cmine = st.enter_context(nc.sbuf_tensor("cmine", [128, 24], F32))

    stA = ExitStack()
    invf = stA.enter_context(nc.sbuf_tensor("invf_s", [128, 2], F32))
    invf2p = stA.enter_context(nc.sbuf_tensor("invf2p", [128, 2], F32))
    taps = stA.enter_context(nc.sbuf_tensor("taps_s", [128, 48 * 4], F32))
    tapsn = stA.enter_context(nc.sbuf_tensor("tapsn", [128, 48 * 4], F32))
    wab = stA.enter_context(nc.sbuf_tensor("wab_s", [128, KC * 12], BF16))
    beta = stA.enter_context(nc.sbuf_tensor("beta_s", [NH, T], F32))
    b_row = stA.enter_context(nc.sbuf_tensor("b_row", [1, T], F32))
    ca_row = stA.enter_context(nc.sbuf_tensor("ca_row", [1, T], F32))
    a_row1 = stA.enter_context(nc.sbuf_tensor("a_row1", [1, T], F32))
    b_row1 = stA.enter_context(nc.sbuf_tensor("b_row1", [1, T], F32))
    ca_row1 = stA.enter_context(nc.sbuf_tensor("ca_row1", [1, T], F32))
    a_rowp = [a_row, a_row1]
    b_rowp = [b_row, b_row1]
    ca_rowp = [ca_row, ca_row1]
    y_sb = stA.enter_context(nc.sbuf_tensor("y_sb", [128, 24], F32))
    q_sb = stA.enter_context(nc.sbuf_tensor("q_sb", [1, 24], F32))

    with nc.Block(no_gpsimd_drain=True) as blk:
        @blk.sync
        def _(sync):
            p.inc(sync.dma_start(out=consts[:], in_=const_h.ap()), "sy", 16)
            p.inc(sync.dma_start(out=invf[:], in_=invf_c.ap()), "sy", 16)
            p.inc(sync.dma_start(out=ones_f[:], in_=ones_row_f.ap()), "sy", 16)
            p.inc(sync.dma_start(out=ones_b[:], in_=ones_col_b.ap()), "sy", 16)
            p.inc(sync.dma_start(out=taps[:].rearrange("p (c j) -> p c j", j=4),
                                 in_=taps_h.transpose([1, 0, 2])), "sy", 16)
            p.inc(sync.dma_start(out=rmsw[:], in_=rmsw_h), "sy", 16)
            p.inc(sync.dma_start(out=gsc[:], in_=gsc_h), "sy", 16)
            p.inc(sync.dma_start(out=usc[:], in_=usc_h), "sy", 16)
            p.inc(sync.dma_start(out=dsc[:], in_=dsc_h), "sy", 16)
            p.inc(sync.dma_start(out=rmask[:], in_=rmask_h), "sy", 16)

        @blk.gpsimd
        def _(gp):
            p.inc(gp.dma_start(out=wab[:].rearrange("p (kc n) -> p kc n", n=12),
                               in_=wab_h.rearrange("(kc p) n -> kc p n", p=128).transpose([1, 0, 2])), "po", 16)

    sy_consts = p.cnt["sy"]

    with nc.Block(no_gpsimd_drain=True) as blk:
        @blk.vector
        def _(ve):
            p.wait(ve, "sy", sy_consts)
            ve.tensor_scalar(tapsn[:], taps[:], -1.0, None, ALU.mult)
            ve.tensor_scalar(dscn[:], dsc[:], -128.0, None, ALU.mult)
            p.inc(ve.tensor_scalar(invf2p[:], invf[:], 1.0 / TWO_PI, None, ALU.mult), "ve")

    emit_ln(p, x_h, consts, ones_f, ones_b, xn)
    ve_xn = p.cnt["ve"]

    bst = {}
    sgst = {}

    wqkv = stA.enter_context(nc.sbuf_tensor("wqkv_s", [128, KC * 1024], BF16))
    wg = stA.enter_context(nc.sbuf_tensor("wg_s", [128, KC * DH], BF16))
    if True:
        wqkv_done = {}
        wg_done = {}

        with nc.Block(no_gpsimd_drain=True) as blk:
            @blk.gpsimd
            def _(gp):
                wqkv_done[0] = p.inc(
                    gp.dma_start(out=wqkv[:].rearrange("p (kc n) -> p kc n", n=1024),
                                 in_=wqkv_h[0].rearrange("(kc p) n -> kc p n", p=128).transpose([1, 0, 2])), "po", 16)
                wg_done[0] = p.inc(
                    gp.dma_start(out=wg[:].rearrange("p (kc n) -> p kc n", n=DH),
                                 in_=wg_h[0].rearrange("(kc p) n -> kc p n", p=128).transpose([1, 0, 2])), "po", 16)

        with nc.psum_tensor("ps_ab", [NH, 2 * T], F32) as ps_ab, nc.Block(no_gpsimd_drain=True) as blk:
            @blk.tensor
            def _(te):
                p.wait(te, "ve", ve_xn)
                p.wait(te, "po", 16)
                last = None
                for which in range(2):
                    for (o, n) in pieces(T):
                        for kc in range(KC):
                            last = te.matmul(ps_ab[0:NH, which * T + o: which * T + o + n],
                                             wab[:, kc * 12 + which * NH: kc * 12 + (which + 1) * NH],
                                             xn[:, kc * TH + 3 + o: kc * TH + 3 + o + n],
                                             start=(kc == 0), stop=(kc == KC - 1))
                p.inc(last, "pe")

            pe_ab = p.cnt["pe"]

            @blk.scalar
            def _(ac):
                p.wait(ac, "pe", pe_ab)
                ac.activation(alpha[:, :], ps_ab[0:NH, 0:T], AF.Sigmoid)
                p.inc(ac.activation(beta[:, :], ps_ab[0:NH, T:2 * T], AF.Sigmoid), "ac")

        ac_ab = p.cnt["ac"]

        with (
            nc.sbuf_tensor("z_sb", [128, 8 * TH], BF16) as z_sb,
            nc.sbuf_tensor("vc", [128, 4 * T], BF16) as vc,
            nc.sbuf_tensor("dpre", [128, 2 * T], BF16) as dpre,
            nc.sbuf_tensor("rrn", [128, T], F32) as rrn,
            nc.sbuf_tensor("scr2", [128, T], F32) as scr2,
            nc.sbuf_tensor("sinb", [128, 2 * T], F32) as sinb,
            nc.sbuf_tensor("cosb", [128, 2 * T], F32) as cosb,
            nc.sbuf_tensor("b_b", [128, 4 * T], BF16) as b_b,
            nc.sbuf_tensor("ab_sb", [128, T], F32) as ab_sb,
            nc.sbuf_tensor("u0", [128, T], F32) as u0,
            nc.sbuf_tensor("sg_sb", [128, 4 * T], BF16) as sg_sb,
        ):
            def tap_col(h, cc, j, neg=False):
                t = tapsn if neg else taps
                idx = (h * 8 + cc) * 4 + j
                return t[:, idx:idx + 1]

            h4p1 = {}
            pe_rep_h = {}
            ve_b_h = {}
            for h in range(NH):
                with nc.psum_tensor(f"ps_z{h}", [128, 2 * THP], F32) as ps_z, nc.Block(no_gpsimd_drain=True) as blk:
                    pz_done = {}

                    @blk.tensor
                    def _(te):
                        p.wait(te, "po", wqkv_done[h])
                        if h == 0:
                            p.wait(te, "ve")
                        else:
                            p.wait(te, "cv", h4p1[h - 1])
                        p.wait(te, "gs")
                        for cc in range(8):
                            par = cc % 2
                            if cc >= 2:
                                p.wait(te, "ez", 8 * h + cc - 1)
                            last = None
                            for (o, n) in pieces(TH):
                                for kc in range(KC):
                                    last = te.matmul(ps_z[:, par * THP + o: par * THP + o + n],
                                                     wqkv[:, kc * 1024 + cc * 128: kc * 1024 + cc * 128 + 128],
                                                     xn[:, kc * TH + o: kc * TH + o + n],
                                                     start=(kc == 0), stop=(kc == KC - 1))
                            pz_done[cc] = p.inc(last, "pe")
                        p.wait(te, "ez", 8 * h + 8)

                    @blk.vector
                    def _(ve):
                        for cc in range(8):
                            par = cc % 2
                            p.wait(ve, "pe", pz_done[cc])
                            p.inc(ve.tensor_copy(z_sb[:, cc * TH:(cc + 1) * TH],
                                                 ps_z[:, par * THP: par * THP + TH]), "ez")

                    @blk.gpsimd
                    def _(gp):
                        if h + 1 < NH:
                            p.wait(gp, "pe", pz_done[7])
                            wqkv_done[h + 1] = p.inc(
                                gp.dma_start(out=wqkv[:].rearrange("p (kc n) -> p kc n", n=1024),
                                             in_=wqkv_h[h + 1].rearrange("(kc p) n -> kc p n", p=128).transpose([1, 0, 2])),
                                "po", 16)

                with nc.Block(no_gpsimd_drain=True) as blk:
                    @blk.vector
                    def _(ve):
                        for jc in range(2):
                            dsl = slice(jc * T, (jc + 1) * T)

                            def zk(j, jc=jc):
                                return z_sb[:, (2 + jc) * TH + j: (2 + jc) * TH + j + T]

                            def zq(j, jc=jc):
                                return z_sb[:, (0 + jc) * TH + j: (0 + jc) * TH + j + T]

                            ve.tensor_scalar(dpre[:, dsl], zk(0), tap_col(h, 2 + jc, 0), None, ALU.mult)
                            for j in range(1, 4):
                                ve.scalar_tensor_tensor(dpre[:, dsl], zk(j), tap_col(h, 2 + jc, j),
                                                        dpre[:, dsl], ALU.mult, ALU.add)
                            for j in range(4):
                                ve.scalar_tensor_tensor(dpre[:, dsl], zq(j), tap_col(h, 0 + jc, j, neg=True),
                                                        dpre[:, dsl], ALU.mult, ALU.add)
                        for cc in range(4):
                            vsl = slice(cc * T, (cc + 1) * T)

                            def zv(j, cc=cc):
                                return z_sb[:, (4 + cc) * TH + j: (4 + cc) * TH + j + T]

                            ve.tensor_scalar(vc[:, vsl], zv(0), tap_col(h, 4 + cc, 0), None, ALU.mult)
                            for j in range(1, 4):
                                ve.scalar_tensor_tensor(vc[:, vsl], zv(j), tap_col(h, 4 + cc, j),
                                                        vc[:, vsl], ALU.mult, ALU.add)
                        last = None
                        for jc in range(2):
                            dsl = slice(jc * T, (jc + 1) * T)
                            ve.tensor_scalar(rrn[:, :], dpre[:, dsl], invf2p[:, jc:jc + 1], MAGIC,
                                             ALU.mult, ALU.add)
                            ve.tensor_scalar(rrn[:, :], rrn[:, :], MAGIC, None, ALU.subtract)
                            ve.tensor_scalar(rrn[:, :], rrn[:, :], TWO_PI, None, ALU.mult)
                            last = ve.scalar_tensor_tensor(sinb[:, dsl], dpre[:, dsl], invf[:, jc:jc + 1],
                                                           rrn[:, :], ALU.mult, ALU.subtract)
                        p.inc(last, "ve")

                ve_rr = p.cnt["ve"]

                with nc.Block(no_gpsimd_drain=True) as blk:
                    @blk.scalar
                    def _(ac):
                        p.wait(ac, "ve", ve_rr)
                        last = None
                        for jc in range(2):
                            dsl = slice(jc * T, (jc + 1) * T)
                            ac.activation(cosb[:, dsl], sinb[:, dsl], AF.Abs)
                            ac.activation(cosb[:, dsl], cosb[:, dsl], AF.Sin,
                                          scale=consts[:, C_NEG1:C_NEG1 + 1],
                                          bias=consts[:, C_HALFPI:C_HALFPI + 1])
                            last = ac.activation(sinb[:, dsl], sinb[:, dsl], AF.Sin)
                        p.inc(last, "ac")

                ac_trig = p.cnt["ac"]

                a_r = a_rowp[h % 2]
                b_r = b_rowp[h % 2]
                ca_r = ca_rowp[h % 2]
                with nc.psum_tensor(f"ps_r{h}", [128, 2 * T], F32) as ps_r:
                    with nc.Block(no_gpsimd_drain=True) as blk:
                        @blk.gpsimd
                        def _(gp):
                            p.wait(gp, "ac", ac_ab)
                            if h >= 2:
                                p.wait(gp, "ve", ve_b_h[h - 2])
                                p.wait(gp, "pe", pe_rep_h[h - 2])
                            p.inc(gp.dma_start(out=a_r[:], in_=alpha[h:h + 1, :]), "sy", 16)
                            p.inc(gp.dma_start(out=b_r[:], in_=beta[h:h + 1, :]), "sy", 16)

                    sy_rows = p.cnt["sy"]

                    with nc.Block(no_gpsimd_drain=True) as blk:
                        @blk.tensor
                        def _(te):
                            p.wait(te, "sy", sy_rows)
                            p.wait(te, "gs")
                            last = None
                            for which, src in ((0, b_r), (1, a_r)):
                                for (o, n) in pieces(T):
                                    last = te.matmul(ps_r[:, which * T + o: which * T + o + n],
                                                     ones_f[0:1, :], src[0:1, o:o + n],
                                                     start=True, stop=True)
                            p.inc(last, "pe")

                        pe_rep = p.cnt["pe"]
                        pe_rep_h[h] = pe_rep

                        @blk.vector
                        def _(ve):
                            p.wait(ve, "ac", ac_trig)
                            p.wait(ve, "pe", pe_rep)
                            ve.tensor_tensor_scan(ca_r[0:1, :], a_r[0:1, :], a_r[0:1, :],
                                                  1.0, ALU.mult, ALU.bypass)
                            for cc in range(4):
                                ve.tensor_copy(q_sb[0:1, h * 4 + cc: h * 4 + cc + 1],
                                               ca_r[0:1, T - 1:T])
                            for jc in range(2):
                                dsl = slice(jc * T, (jc + 1) * T)
                                ve.tensor_tensor(cosb[:, dsl], cosb[:, dsl], ps_r[:, 0:T], ALU.mult)
                                ve.tensor_tensor(sinb[:, dsl], sinb[:, dsl], ps_r[:, 0:T], ALU.mult)
                            p.inc(ve.tensor_copy(ab_sb[:, :], ps_r[:, T:2 * T]), "cv")

                        h4p1[h] = p.cnt["cv"]

                        @blk.vector
                        def _(ve):
                            p.wait(ve, "so")
                            for jc in range(2):
                                dsl = slice(jc * T, (jc + 1) * T)
                                e = slice(jc * T, (jc + 1) * T)
                                o_ = slice((2 + jc) * T, (3 + jc) * T)
                                ve.tensor_tensor(rrn[:, :], vc[:, e], cosb[:, dsl], ALU.mult)
                                ve.tensor_tensor(scr2[:, :], vc[:, o_], sinb[:, dsl], ALU.mult)
                                ve.tensor_tensor(b_b[:, e], rrn[:, :], scr2[:, :], ALU.subtract)
                                ve.tensor_tensor(rrn[:, :], vc[:, o_], cosb[:, dsl], ALU.mult)
                                ve.tensor_tensor(scr2[:, :], vc[:, e], sinb[:, dsl], ALU.mult)
                                ve.tensor_tensor(b_b[:, o_], rrn[:, :], scr2[:, :], ALU.add)
                            last = None
                            for cc in range(4):
                                bsl = slice(cc * T, (cc + 1) * T)
                                ve.tensor_tensor_scan(u0[:, :], ab_sb[:, :], b_b[:, bsl],
                                                      0.0, ALU.mult, ALU.add)
                                last = ve.tensor_copy(y_sb[:, h * 4 + cc: h * 4 + cc + 1], u0[:, T - 1:T])
                            p.inc(last, "ve")

                        ve_b = p.cnt["ve"]
                        ve_b_h[h] = ve_b

                        @blk.sync
                        def _(sync):
                            p.wait(sync, "ve", ve_b)
                            bst[h] = p.inc(sync.dma_start(out=b_scr[:, h * 4 * T:(h * 4 + 4) * T], in_=b_b[:, :]), "so", 16)

                with nc.psum_tensor(f"ps_g{h}", [128, 2 * T], F32) as ps_g, nc.Block(no_gpsimd_drain=True) as blk:
                    gc_done = {}

                    @blk.tensor
                    def _(te):
                        p.wait(te, "po", wg_done[h])
                        for cc in range(4):
                            par = cc % 2
                            if cc >= 2:
                                p.wait(te, "gs", 4 * h + cc - 1)
                            last = None
                            for (o, n) in pieces(T):
                                for kc in range(KC):
                                    last = te.matmul(ps_g[:, par * T + o: par * T + o + n],
                                                     wg[:, kc * DH + cc * 128: kc * DH + cc * 128 + 128],
                                                     xn[:, kc * TH + 3 + o: kc * TH + 3 + o + n],
                                                     start=(kc == 0), stop=(kc == KC - 1))
                            gc_done[cc] = p.inc(last, "pe")
                        p.wait(te, "gs", 4 * h + 4)

                    @blk.scalar
                    def _(ac):
                        p.wait(ac, "so")
                        for cc in range(4):
                            par = cc % 2
                            p.wait(ac, "pe", gc_done[cc])
                            p.inc(ac.activation(sg_sb[:, cc * T:(cc + 1) * T],
                                                ps_g[:, par * T: par * T + T], AF.Silu), "gs")

                    @blk.sync
                    def _(sync):
                        p.wait(sync, "gs", 4 * h + 4)
                        sgst[h] = p.inc(sync.dma_start(out=sg_scr[:, h * 4 * T:(h * 4 + 4) * T], in_=sg_sb[:, :]), "so", 16)

                    @blk.gpsimd
                    def _(gp):
                        if h + 1 < NH:
                            p.wait(gp, "pe", gc_done[3])
                            wg_done[h + 1] = p.inc(
                                gp.dma_start(out=wg[:].rearrange("p (kc n) -> p kc n", n=DH),
                                             in_=wg_h[h + 1].rearrange("(kc p) n -> kc p n", p=128).transpose([1, 0, 2])),
                                "po", 16)

        with nc.Block(no_gpsimd_drain=True) as blk:
            @blk.sync
            def _(sync):
                p.wait(sync, "ve")
                p.inc(sync.dma_start(out=cc_in[0:128, :], in_=y_sb[:]), "so", 16)
                p.inc(sync.dma_start(out=cc_in[128:129, :], in_=q_sb[:]), "so", 16)

        so_ccin = p.cnt["so"]

        with nc.Block(no_gpsimd_drain=True) as blk:
            @blk.gpsimd
            def _(gp):
                p.wait(gp, "so", so_ccin)
                gp.collective_compute(
                    "AllGather",
                    ALU.bypass,
                    replica_groups=[list(range(NCORE))],
                    ins=[cc_in[:]],
                    outs=[cc_out[:]],
                ).then_inc(p.sem["cc"], 1)

        with nc.Block(no_gpsimd_drain=True) as blk:
            @blk.sync
            def _(sync):
                p.wait(sync, "cc", 1)
                for r in range(NCORE):
                    p.inc(sync.dma_start(out=ysb[:, r * 24:(r + 1) * 24],
                                         in_=cc_out[r * 129:r * 129 + 128, :]), "sy", 16)
                    p.inc(sync.dma_start(out=qsb[0:1, r * 24:(r + 1) * 24],
                                         in_=cc_out[r * 129 + 128:r * 129 + 129, :]), "sy", 16)

        sy_carry = p.cnt["sy"]

    stA.close()
    acc = st.enter_context(nc.sbuf_tensor("acc", [128, KC * T], F32))

    with (
        nc.sbuf_tensor("cbuf", [128, 2 * 24], F32) as cbuf,
        nc.psum_tensor("ps_q", [128, NCORE * 32], F32) as ps_q,
        nc.Block(no_gpsimd_drain=True) as blk,
    ):
        @blk.tensor
        def _(te):
            p.wait(te, "sy", sy_carry)
            last = None
            for r in range(NCORE):
                last = te.matmul(ps_q[:, r * 32:r * 32 + 24],
                                 ones_f[0:1, :],
                                 qsb[0:1, r * 24:(r + 1) * 24],
                                 start=True, stop=True)
            p.inc(last, "pe")

        pe_q = p.cnt["pe"]

        @blk.vector
        def _(ve):
            p.wait(ve, "pe", pe_q)
            cprev, cnext = cbuf[:, 0:24], cbuf[:, 24:48]
            ve.memset(cprev, 0.0)
            ve.memset(cmine[:], 0.0)
            last = None
            for r in range(NCORE):
                last = ve.scalar_tensor_tensor(cmine[:], cprev, rmask[:, r:r + 1],
                                               cmine[:], ALU.mult, ALU.add)
                if r < NCORE - 1:
                    ve.tensor_tensor(cnext, cprev, ps_q[:, r * 32:r * 32 + 24], ALU.mult)
                    last = ve.tensor_add(cnext, cnext, ysb[:, r * 24:(r + 1) * 24])
                    cprev, cnext = cnext, cprev
            p.inc(last, "ve")

    bsg_done = {}
    sg_done = {}
    scan_done = {}
    o_done = {}
    with nc.sbuf_tensor("o_sb", [128, 24 * T], BF16) as o_sb:
        with (
            nc.sbuf_tensor("b_c0", [128, T], BF16) as b_c0,
            nc.sbuf_tensor("b_c1", [128, T], BF16) as b_c1,
            nc.sbuf_tensor("sg_f", [128, 4 * T], BF16) as sg_f,
            nc.sbuf_tensor("ab2_sb", [128, T], F32) as ab2_sb,
            nc.sbuf_tensor("u_sb", [128, 4 * T], F32) as u_sb,
            nc.sbuf_tensor("u2s", [128, T], BF16) as u2s,
            nc.sbuf_tensor("rrow", [1, T], F32) as rrow,
            nc.sbuf_tensor("oscr", [128, T], F32) as oscr,
        ):
            b_c = [b_c0, b_c1]

            with nc.Block(no_gpsimd_drain=True) as blk:
                @blk.sync
                def _(sync):
                    p.wait(sync, "so", bst[0])
                    bsg_done[0] = p.inc(sync.dma_start(out=b_c[0][:], in_=b_scr[:, 0:T]), "sy", 16)
                    p.wait(sync, "so", sgst[0])
                    sg_done[0] = p.inc(sync.dma_start(out=sg_f[:], in_=sg_scr[:, 0:4 * T]), "sy", 16)

            for h in range(NH):
                with (
                    nc.psum_tensor(f"ps_r2{h}", [128, T], F32) as ps_r2,
                    nc.psum_tensor(f"ps_m{h}", [1, T], F32) as ps_m,
                ):
                    with nc.Block(no_gpsimd_drain=True) as blk:
                        @blk.sync
                        def _(sync):
                            p.wait(sync, "ac", ac_ab)
                            p.wait(sync, "pe")
                            p.inc(sync.dma_start(out=a_row[:], in_=alpha[h:h + 1, :]), "sy", 16)

                    sy_ar = p.cnt["sy"]

                    with nc.Block(no_gpsimd_drain=True) as blk:
                        @blk.tensor
                        def _(te):
                            p.wait(te, "sy", sy_ar)
                            p.wait(te, "ve")
                            last = None
                            for (o, n) in pieces(T):
                                last = te.matmul(ps_r2[:, o:o + n], ones_f[0:1, :],
                                                 a_row[0:1, o:o + n], start=True, stop=True)
                            p.inc(last, "pe")

                        pe_ar = p.cnt["pe"]

                        @blk.vector
                        def _(ve):
                            p.wait(ve, "pe", pe_ar)
                            p.inc(ve.tensor_copy(ab2_sb[:, :], ps_r2[:, :]), "ve")

                    for cc in range(4):
                        gi = h * 4 + cc
                        par = gi % 2
                        with nc.Block(no_gpsimd_drain=True) as blk:
                            @blk.sync
                            def _(sync):
                                if gi + 1 < 24:
                                    if gi >= 1:
                                        p.wait(sync, "ve", scan_done[gi - 1])
                                    p.wait(sync, "so", bst[(gi + 1) // 4])
                                    bsg_done[gi + 1] = p.inc(
                                        sync.dma_start(out=b_c[(gi + 1) % 2][:],
                                                       in_=b_scr[:, (gi + 1) * T:(gi + 2) * T]), "sy", 16)

                            @blk.vector
                            def _(ve):
                                p.wait(ve, "sy", bsg_done[gi])
                                scan_done[gi] = p.inc(
                                    ve.tensor_tensor_scan(u_sb[:, cc * T:(cc + 1) * T],
                                                          ab2_sb[:, :], b_c[par][:, :],
                                                          cmine[:, gi:gi + 1],
                                                          ALU.mult, ALU.add), "ve")

                    ve_u = p.cnt["ve"]

                    with nc.Block(no_gpsimd_drain=True) as blk:
                        pe_base = p.cnt["pe"]
                        gs_marks = {}

                        @blk.scalar
                        def _(ac):
                            p.wait(ac, "ve", ve_u)
                            for cc in range(4):
                                if cc >= 1:
                                    p.wait(ac, "pe", pe_base + cc)
                                gs_marks[cc] = p.inc(
                                    ac.activation(u2s[:, :], u_sb[:, cc * T:(cc + 1) * T], AF.Square), "gs")

                        @blk.tensor
                        def _(te):
                            for cc in range(4):
                                p.wait(te, "gs", gs_marks[cc])
                                last = None
                                for (o, n) in pieces(T):
                                    last = te.matmul(ps_m[0:1, o:o + n], ones_b[:, :],
                                                     u2s[:, o:o + n],
                                                     start=(cc == 0), stop=(cc == 3))
                                p.inc(last, "pe")

                    pe_m = p.cnt["pe"]

                    with nc.Block(no_gpsimd_drain=True) as blk:
                        @blk.scalar
                        def _(ac):
                            p.wait(ac, "pe", pe_m)
                            ac.activation(rrow[0:1, :], ps_m[0:1, 0:T], AF.Ln,
                                          scale=consts[0:1, C_INV512:C_INV512 + 1],
                                          bias=consts[0:1, C_EPSRMS:C_EPSRMS + 1])
                            p.inc(ac.activation(rrow[0:1, :], rrow[0:1, :], AF.Exp,
                                                scale=consts[0:1, C_NEGHALF:C_NEGHALF + 1]), "ac")

                    ac_rs = p.cnt["ac"]

                    with nc.Block(no_gpsimd_drain=True) as blk:
                        @blk.tensor
                        def _(te):
                            p.wait(te, "ac", ac_rs)
                            last = None
                            for (o, n) in pieces(T):
                                last = te.matmul(ps_r2[:, o:o + n], ones_f[0:1, :],
                                                 rrow[0:1, o:o + n], start=True, stop=True)
                            p.inc(last, "pe")

                        pe_rs = p.cnt["pe"]

                        @blk.vector
                        def _(ve):
                            p.wait(ve, "pe", pe_rs)
                            p.wait(ve, "sy", sg_done[h])
                            last = None
                            for cc in range(4):
                                gi = h * 4 + cc
                                ve.scalar_tensor_tensor(oscr[:, :], u_sb[:, cc * T:(cc + 1) * T],
                                                        rmsw[:, cc:cc + 1], ps_r2[:, :],
                                                        ALU.mult, ALU.mult)
                                last = ve.tensor_tensor(o_sb[:, gi * T:(gi + 1) * T],
                                                        oscr[:, :], sg_f[:, cc * T:(cc + 1) * T], ALU.mult)
                            o_done[h] = p.inc(last, "ve")

                        @blk.sync
                        def _(sync):
                            if h + 1 < NH:
                                p.wait(sync, "ve", o_done[h])
                                p.wait(sync, "so", sgst[h + 1])
                                sg_done[h + 1] = p.inc(
                                    sync.dma_start(out=sg_f[:],
                                                   in_=sg_scr[:, (h + 1) * 4 * T:(h + 2) * 4 * T]), "sy", 16)

        ve_o = p.cnt["ve"]

        wo_done = {}
        wo_pe_half = {}
        with (
            nc.sbuf_tensor("wo_sb0", [128, 12 * 512], BF16) as wo_sb0,
            nc.sbuf_tensor("wo_sb1", [128, 12 * 512], BF16) as wo_sb1,
            nc.sbuf_tensor("x2a", [128, T], F32) as x2a,
            nc.sbuf_tensor("x2b", [128, T], F32) as x2b,
        ):
            x2 = [x2a, x2b]
            wo_b = [wo_sb0, wo_sb1]

            with nc.Block(no_gpsimd_drain=True) as blk:
                @blk.gpsimd
                def _(gp):
                    for ch in range(2):
                        wo_done[ch] = p.inc(
                            gp.dma_start(out=wo_b[ch][:].rearrange("p (kc n) -> p kc n", n=512),
                                         in_=wo_h[0, ch * 12:(ch + 1) * 12].transpose([1, 0, 2])),
                            "po", 16)

            cv_wo_base = p.cnt["cv"]
            for mcb in range(4):
                with nc.psum_tensor(f"ps_o{mcb}", [128, 4 * T], F32) as ps_o, nc.Block(no_gpsimd_drain=True) as blk:
                    @blk.tensor
                    def _(te):
                        if mcb == 0:
                            p.wait(te, "ve", ve_o)
                        if mcb >= 1:
                            p.wait(te, "cv", cv_wo_base + 4 * mcb)
                        for half in range(2):
                            ch = 2 * mcb + half
                            p.wait(te, "po", wo_done[ch])
                            last = None
                            for mi in range(4):
                                for (o, n) in pieces(T):
                                    for k2 in range(12):
                                        kc = half * 12 + k2
                                        last = te.matmul(ps_o[:, mi * T + o: mi * T + o + n],
                                                         wo_b[ch % 2][:, k2 * 512 + mi * 128: k2 * 512 + mi * 128 + 128],
                                                         o_sb[:, kc * T + o: kc * T + o + n],
                                                         start=(kc == 0), stop=(kc == 23))
                            wo_pe_half[ch] = p.inc(last, "pe")

                    pe_o = p.cnt["pe"]

                    sy_x2m = {}

                    @blk.sync
                    def _(sync):
                        for mi in range(4):
                            mc = mcb * 4 + mi
                            if mc >= 2:
                                p.wait(sync, "cv", cv_wo_base + mc - 1)
                            sy_x2m[mi] = p.inc(sync.dma_start(out=x2[mi % 2][:],
                                                 in_=x_h[mc * 128:(mc + 1) * 128, 3:TH]), "sy", 16)

                    @blk.gpsimd
                    def _(gp):
                        for half in range(2):
                            ch = 2 * (mcb + 1) + half
                            if ch < 8:
                                p.wait(gp, "pe", wo_pe_half[ch - 2])
                                wo_done[ch] = p.inc(
                                    gp.dma_start(out=wo_b[ch % 2][:].rearrange("p (kc n) -> p kc n", n=512),
                                                 in_=wo_h[mcb + 1, half * 12:(half + 1) * 12].transpose([1, 0, 2])),
                                    "po", 16)

                    @blk.vector
                    def _(ve):
                        p.wait(ve, "pe", pe_o)
                        for mi in range(4):
                            mc = mcb * 4 + mi
                            p.wait(ve, "sy", sy_x2m[mi])
                            p.inc(ve.tensor_tensor(acc[:, mc * T:(mc + 1) * T],
                                                   x2[mi % 2][:, :], ps_o[:, mi * T: mi * T + T],
                                                   ALU.add), "cv")

    cv_wo_end = p.cnt["cv"]

    NIG = 32
    gu_done = {}
    d_done = {}
    pg_done_hist = {}
    ped_hist = {}
    so_out_base = p.cnt["so"]
    with (
        nc.sbuf_tensor("graw0", [128, 16 * 256], BF16) as graw0,
        nc.sbuf_tensor("graw1", [128, 16 * 256], BF16) as graw1,
        nc.sbuf_tensor("uraw0", [128, 16 * 256], BF16) as uraw0,
        nc.sbuf_tensor("uraw1", [128, 16 * 256], BF16) as uraw1,
        nc.sbuf_tensor("sgt", [128, 2 * T], BF16) as sgt,
        nc.sbuf_tensor("h_sb", [128, 16 * T], BF16) as h_sb,
        nc.sbuf_tensor("draw0", [128, 16 * 512], BF16) as draw0,
        nc.sbuf_tensor("draw1", [128, 16 * 512], BF16) as draw1,
    ):
        s_row = sgt[:].bitcast(F32)[0:1, 0:T]
        graw = [graw0, graw1]
        uraw = [uraw0, uraw1]
        draw = [draw0, draw1]
        outs = [graw0[:].bitcast(F32)[:, 0:T], uraw0[:].bitcast(F32)[:, 0:T]]

        with nc.Block(no_gpsimd_drain=True) as blk:
            @blk.gpsimd
            def _(gp):
                p.inc(gp.dma_start(out=graw[0][:].rearrange("p (kc n) -> p kc n", n=256),
                                   in_=gate_h[0].transpose([1, 0, 2])), "po", 16)
                gu_done[0] = p.inc(gp.dma_start(out=uraw[0][:].rearrange("p (kc n) -> p kc n", n=256),
                                                in_=up_h[0].transpose([1, 0, 2])), "po", 16)

        ve_out_marks = {}

        def emit_down(s, mcb):
            dmai = s * 4 + mcb
            par = dmai % 2
            with nc.psum_tensor(f"ps_d{dmai}", [128, 2 * T], F32) as ps_d, nc.Block(no_gpsimd_drain=True) as blk:
                pe_base = p.cnt["pe"]
                ve_prev = p.cnt["ve"]
                ve2_marks = {}

                @blk.gpsimd
                def _(gp):
                    if dmai >= 2:
                        p.wait(gp, "pe", ped_hist[dmai - 2])
                    d_done[dmai] = p.inc(
                        gp.dma_start(out=draw[par][:].rearrange("p (kc n) -> p kc n", n=512),
                                     in_=down_h[mcb, s * 16:(s + 1) * 16].transpose([1, 0, 2])),
                        "po", 16)

                @blk.vector
                def _(ve2):
                    for j in range(2):
                        p.wait(ve2, "pe", pe_base + j + 1)
                        last = None
                        for mi in (2 * j, 2 * j + 1):
                            mc = mcb * 4 + mi
                            asl = slice(mc * T, (mc + 1) * T)
                            if s < 3:
                                last = ve2.scalar_tensor_tensor(acc[:, asl],
                                                                ps_d[:, (mi % 2) * T:(mi % 2) * T + T],
                                                                dsc[:, mc:mc + 1], acc[:, asl],
                                                                ALU.mult, ALU.add)
                            else:
                                if mc >= 2:
                                    p.wait(ve2, "so", so_out_base + 16 * (mc - 1))
                                ve_out_marks[mc] = p.inc(
                                    ve2.scalar_tensor_tensor(outs[mi % 2],
                                                             ps_d[:, (mi % 2) * T:(mi % 2) * T + T],
                                                             dsc[:, mc:mc + 1], acc[:, asl],
                                                             ALU.mult, ALU.add), "ve")
                        if s < 3:
                            ve2_marks[j] = p.inc(last, "ve")
                        else:
                            ve2_marks[j] = ve_out_marks[mcb * 4 + 2 * j + 1]

                @blk.tensor
                def _(te):
                    p.wait(te, "po", d_done[dmai])
                    p.wait(te, "ve", ve_prev)
                    for j in range(2):
                        if j == 1:
                            p.wait(te, "ve", ve2_marks[0])
                        last = None
                        for mi in (2 * j, 2 * j + 1):
                            for (o, n) in pieces(T):
                                for kc2 in range(16):
                                    last = te.matmul(ps_d[:, (mi % 2) * T + o:(mi % 2) * T + o + n],
                                                     draw[par][:, kc2 * 512 + mi * 128: kc2 * 512 + mi * 128 + 128],
                                                     h_sb[:, kc2 * T + o: kc2 * T + o + n],
                                                     start=(kc2 == 0), stop=(kc2 == 15))
                        p.inc(last, "pe")

                if s == 3:
                    @blk.sync
                    def _(sync):
                        for mi in range(4):
                            mc = mcb * 4 + mi
                            p.wait(sync, "ve", ve_out_marks[mc])
                            p.inc(sync.dma_start(out=out_h[mc * 128:(mc + 1) * 128, :],
                                                 in_=outs[mi % 2]), "so", 16)

                ped_hist[dmai] = pe_base + 2

        hmark = {}
        ve_S = None
        with nc.psum_tensor("ps_S", [1, T], F32) as ps_S:
            for s in range(4):
                for i8 in range(8):
                    ig = s * 8 + i8
                    par = ig % 2
                    with nc.psum_tensor(f"ps_gu{ig}", [128, 2 * T], F32) as ps_gu, nc.Block(no_gpsimd_drain=True) as blk:
                        pe_base = p.cnt["pe"]
                        pg = {(0, 0): pe_base + 1, (1, 0): pe_base + 2,
                              (0, 1): pe_base + 3, (1, 1): pe_base + 4}
                        gs_prev = p.cnt["gs"]
                        ve_prev = p.cnt["ve"]
                        gs_marks = {}
                        ve_marks = {}

                        @blk.scalar
                        def _(ac):
                            p.wait(ac, "ve", ve_prev)
                            for icp in range(2):
                                ic = 2 * ig + icp
                                p.wait(ac, "pe", pg[(0, icp)])
                                gs_marks[icp] = p.inc(
                                    ac.activation(sgt[:, icp * T:(icp + 1) * T],
                                                  ps_gu[:, 0:T], AF.Silu,
                                                  scale=gsc[:, ic:ic + 1]), "gs")

                        @blk.vector
                        def _(ve):
                            for icp in range(2):
                                ic = 2 * ig + icp
                                p.wait(ve, "gs", gs_marks[icp])
                                p.wait(ve, "pe", pg[(1, icp)])
                                ve_marks[icp] = p.inc(ve.scalar_tensor_tensor(
                                    h_sb[:, (i8 * 2 + icp) * T:(i8 * 2 + icp + 1) * T],
                                    sgt[:, icp * T:(icp + 1) * T], usc[:, ic:ic + 1],
                                    ps_gu[:, T:2 * T], ALU.mult, ALU.mult), "ve")

                        @blk.tensor
                        def _(te):
                            p.wait(te, "po", gu_done[ig])
                            if ig == 0:
                                p.wait(te, "cv", cv_wo_end)
                            for icp in range(2):
                                for which, w in ((0, graw[par]), (1, uraw[par])):
                                    if icp == 0:
                                        if which == 0:
                                            p.wait(te, "gs", gs_prev)
                                        else:
                                            p.wait(te, "ve", ve_prev)
                                    else:
                                        if which == 0:
                                            p.wait(te, "gs", gs_marks[0])
                                        else:
                                            p.wait(te, "ve", ve_marks[0])
                                    psl = which * T
                                    last = None
                                    for (o, n) in pieces(T):
                                        for kc in range(KC):
                                            last = te.matmul(ps_gu[:, psl + o: psl + o + n],
                                                             w[:, kc * 256 + icp * 128: kc * 256 + icp * 128 + 128],
                                                             xn[:, kc * TH + 3 + o: kc * TH + 3 + o + n],
                                                             start=(kc == 0), stop=(kc == KC - 1))
                                    p.inc(last, "pe")

                        @blk.gpsimd
                        def _(gp):
                            if ig + 1 < NIG:
                                if ig >= 1:
                                    p.wait(gp, "pe", pg_done_hist[ig - 1])
                                p.inc(gp.dma_start(out=graw[(ig + 1) % 2][:].rearrange("p (kc n) -> p kc n", n=256),
                                                   in_=gate_h[ig + 1].transpose([1, 0, 2])), "po", 16)
                                gu_done[ig + 1] = p.inc(
                                    gp.dma_start(out=uraw[(ig + 1) % 2][:].rearrange("p (kc n) -> p kc n", n=256),
                                                 in_=up_h[ig + 1].transpose([1, 0, 2])), "po", 16)

                    pg_done_hist[ig] = pe_base + 4

                hmark[s] = p.cnt["ve"]

                with nc.Block(no_gpsimd_drain=True) as blk:
                    @blk.tensor
                    def _(te):
                        p.wait(te, "ve", hmark[s])
                        last = None
                        for kc2 in range(16):
                            for (o, n) in pieces(T):
                                last = te.matmul(ps_S[0:1, o:o + n], ones_b[:, :],
                                                 h_sb[:, kc2 * T + o: kc2 * T + o + n],
                                                 start=(s == 0 and kc2 == 0),
                                                 stop=(s == 3 and kc2 == 15))
                        if s == 3:
                            p.inc(last, "pe")

                if s < 3:
                    for mcb in range(4):
                        emit_down(s, mcb)
                else:
                    pe_S = p.cnt["pe"]
                    with nc.Block(no_gpsimd_drain=True) as blk:
                        @blk.vector
                        def _(ve):
                            p.wait(ve, "pe", pe_S)
                            p.inc(ve.tensor_copy(s_row[0:1, :], ps_S[0:1, :]), "ve")
                    ve_S = p.cnt["ve"]

        with nc.psum_tensor("ps_srep", [128, T], F32) as ps_srep:
            with nc.Block(no_gpsimd_drain=True) as blk:
                @blk.tensor
                def _(te):
                    p.wait(te, "ve", ve_S)
                    last = None
                    for (o, n) in pieces(T):
                        last = te.matmul(ps_srep[:, o:o + n], ones_f[0:1, :],
                                         s_row[0:1, o:o + n], start=True, stop=True)
                    p.inc(last, "pe")

                pe_srep = p.cnt["pe"]

                @blk.vector
                def _(ve):
                    p.wait(ve, "pe", pe_srep)
                    last = None
                    for mc in range(16):
                        last = ve.scalar_tensor_tensor(acc[:, mc * T:(mc + 1) * T],
                                                       ps_srep[:, :], dscn[:, mc:mc + 1],
                                                       acc[:, mc * T:(mc + 1) * T],
                                                       ALU.mult, ALU.add)
                    p.inc(last, "ve")

            for mcb in range(4):
                emit_down(3, mcb)

        with nc.Block(no_gpsimd_drain=True) as blk:
            @blk.sync
            def _(sync):
                p.wait(sync, "so")

    p.st.close()
    return p.nc



def _prep(inputs):
    x = np.asarray(inputs["L0_attn_hidden_states"]).reshape(8192, D)
    xT = np.ascontiguousarray(x.T)
    x_cores = []
    for c in range(NCORE):
        lo = c * T - 3
        sl = xT[:, max(lo, 0): (c + 1) * T]
        if lo < 0:
            sl = np.concatenate([np.zeros((D, -lo), np.float32), sl], axis=1)
        x_cores.append(np.ascontiguousarray(sl))

    Wq = np.asarray(inputs["L0_attn_Wq"])
    Wk = np.asarray(inputs["L0_attn_Wk"])
    Wv = np.asarray(inputs["L0_attn_Wv"])
    Wg = np.asarray(inputs["L0_attn_Wg"])
    Wo = np.asarray(inputs["L0_attn_Wo"])
    cq = np.asarray(inputs["L0_attn_conv_q"])
    ck = np.asarray(inputs["L0_attn_conv_k"])
    cv = np.asarray(inputs["L0_attn_conv_v"])
    rms_w = np.asarray(inputs["L0_attn_rms_w"])

    even = np.arange(0, DH, 2)
    perm_head = np.concatenate([even, even + 1])
    wqkv = np.empty((NH, D, 1024), np.float32)
    wg = np.empty((NH, D, DH), np.float32)
    taps = np.empty((NH, 8, 128, 4), np.float32)
    for h in range(NH):
        base = h * DH
        wqkv[h] = np.concatenate([Wq[:, base + even], Wk[:, base + even],
                                  Wv[:, base + perm_head]], axis=1)
        wg[h] = Wg[:, base + perm_head]
        taps[h] = np.concatenate([cq[base + even], ck[base + even],
                                  cv[base + perm_head]], axis=0).reshape(8, 128, 4)
    taps = np.ascontiguousarray(taps.reshape(48, 128, 4))
    perm_full = np.concatenate([h * DH + perm_head for h in range(NH)])
    wo_p = Wo[perm_full, :]
    wo_t = np.ascontiguousarray(wo_p.reshape(24, 128, 4, 512).transpose(2, 0, 1, 3))
    wab = np.ascontiguousarray(np.concatenate([np.asarray(inputs["L0_attn_Wa"]),
                                               np.asarray(inputs["L0_attn_Wb"])], axis=1))
    rmsw_cols = np.ascontiguousarray(rms_w[perm_head].reshape(4, 128).T)

    gate_w = np.asarray(inputs["L0_ffn_gate_w"])
    up_w = np.asarray(inputs["L0_ffn_up_w"])
    down_w = np.asarray(inputs["L0_ffn_down_w"])
    gate_t = np.ascontiguousarray(gate_w.T.reshape(KC, 128, 32, 256).transpose(2, 0, 1, 3))
    up_t = np.ascontiguousarray(up_w.T.reshape(KC, 128, 32, 256).transpose(2, 0, 1, 3))
    down_t = np.ascontiguousarray(down_w.T.reshape(64, 128, 4, 512).transpose(2, 0, 1, 3))
    gsc = np.ascontiguousarray(np.asarray(inputs["L0_ffn_gate_scale"]).reshape(64, 128).T)
    usc = np.ascontiguousarray(np.asarray(inputs["L0_ffn_up_scale"]).reshape(64, 128).T)
    dsc = np.ascontiguousarray(np.asarray(inputs["L0_ffn_down_scale"]).reshape(16, 128).T)

    return dict(x_cores=x_cores, wqkv=wqkv, wg=wg, taps=taps, wab=wab,
                wo_t=wo_t, rmsw=rmsw_cols,
                gate_t=gate_t, up_t=up_t, down_t=down_t, gsc=gsc, usc=usc, dsc=dsc)


def kernel(**inputs):
    pr = _prep(inputs)

    nc = build_fused()
    eye = np.eye(NCORE, dtype=np.float32)
    ins = []
    for c in range(NCORE):
        rmask = np.ascontiguousarray(np.broadcast_to(eye[c], (128, NCORE)))
        ins.append({
            "x": pr["x_cores"][c], "wqkv": pr["wqkv"], "wg": pr["wg"],
            "wab": pr["wab"], "taps": pr["taps"],
            "rmask": rmask, "rmsw": pr["rmsw"],
            "gsc": pr["gsc"], "usc": pr["usc"], "dsc": pr["dsc"],
            "gate_t": pr["gate_t"], "up_t": pr["up_t"], "down_t": pr["down_t"],
            "wo_t": pr["wo_t"],
        })
    res = run_bass_kernel_spmd(nc, ins, core_ids=list(range(NCORE)))

    outs = [np.asarray(res.results[c]["out"]) for c in range(NCORE)]
    full = np.concatenate(outs, axis=1)
    return np.ascontiguousarray(full.T).reshape(1, 8192, D).astype(np.float32)
